# version 07 (final)
# speedup vs baseline: 1.1718x; 1.0331x over previous
import numpy as np
import ml_dtypes
from contextlib import ExitStack

import concourse.bass as bass
import concourse.mybir as mybir
from concourse.bass_utils import run_bass_kernel_spmd

F32 = mybir.dt.float32
BF16 = mybir.dt.bfloat16
I8 = mybir.dt.int8
AF = mybir.ActivationFunctionType
ALU = mybir.AluOpType
AX = mybir.AxisListType
BF16NP = ml_dtypes.bfloat16

NC = 8
T = 8192
TL = T // NC
DM = 2048
IF = 8192
LNEPS = 1e-5

CORE_IDS = list(range(NC))
_PROG_CACHE = {}


def _prog(name, builder):
    if name not in _PROG_CACHE:
        _PROG_CACHE[name] = builder()
    return _PROG_CACHE[name]


def _f32(a):
    return np.ascontiguousarray(a, dtype=np.float32)



def emit_ln_stats(s, x_ap, scr_ap, c, eps, sem, n0, a=1.0):
    nf = x_ap.free_size()
    inv = 1.0 / nf
    n = n0

    def step(ins):
        nonlocal n
        ins.then_inc(sem, 1)
        n += 1
        s.wait_ge(sem, n)

    step(s.activation(scr_ap, x_ap, AF.Copy, accum_out=c[:, 0:1]))
    step(s.activation(scr_ap, x_ap, AF.Square, accum_out=c[:, 1:2]))
    step(s.activation(c[:, 2:3], c[:, 0:1], AF.Copy, scale=inv))
    step(s.activation(c[:, 3:4], c[:, 2:3], AF.Square))
    step(s.activation(c[:, 3:4], c[:, 3:4], AF.Copy, scale=-a, bias=eps))
    step(s.activation(c[:, 4:5], c[:, 1:2], AF.Ln, scale=inv * a, bias=c[:, 3:4]))
    step(s.activation(c[:, 5:6], c[:, 4:5], AF.Exp, scale=-0.5))
    return n



def emit_ffn0(nc, ext, tag="", prefill=True):
    sb = lambda n, s, d: nc.sbuf_tensor(tag + n, s, d)
    ps = lambda n, s, d: nc.psum_tensor(tag + n, s, d)
    x_in = ext["x_in"]
    wg8 = ext["wg8"]
    wu8 = ext["wu8"]
    wd8 = ext["wd8"]
    sg_in = ext["sg_in"]
    su_in = ext["su_in"]
    sd_in = ext["sd_in"]
    id_in = ext["id_in"]
    x_out = ext["x_out"]

    NT = TL // 128
    NK = DM // 128
    NQ = IF // 128
    NH = 2
    GRP = 2
    NG = NQ // GRP

    with ExitStack() as ctx:
        ent = ctx.enter_context
        u_fm = ent(sb("u_fm", [128, NK, TL], BF16))
        u_tm = ent(sb("u_tm", [128, DM], BF16))
        xt = ent(sb("xt", [128, 2, DM], F32))
        interm = ent(sb("interm", [128, NQ, 512], BF16))
        wgu8 = ent(sb("wgu8", [128, 2, 2, NK, GRP * 128], I8))
        wbuf = ent(sb("wbuf", [128, 2, 2, NK, 128], BF16))
        wd8s = ent(sb("wd8s", [128, 2, NQ, 128], I8))
        wdb = ent(sb("wdb", [128, 2, NQ, 128], BF16))
        scol = ent(sb("scol", [128, NQ + NQ + NK], F32))
        cidf = ent(sb("cidf", [128, 128], F32))
        idb = ent(sb("idb", [128, 128], BF16))
        sgv = ent(sb("sgv", [128, 2, 512], BF16))
        upv = ent(sb("upv", [128, 2, 512], BF16))
        dfm = ent(sb("dfm", [128, 2, 512], BF16))
        xev = ent(sb("xev", [128, 2, 128], F32))
        cst = ent(sb("cst", [128, 8], F32))
        sqp = ent(sb("sqp", [128, 128], F32))

        pt = ent(ps("pt", [128, 2, 1024], BF16))
        ptd = pt
        pg = ent(ps("pg", [128, 2, 512], F32))
        pu = ent(ps("pu", [128, 2, 512], F32))
        pd = ent(ps("pd", [128, 2, 512], F32))

        s_cid = ent(nc.semaphore("s_cid"))
        s_cq = ent(nc.semaphore("s_cq"))
        s_ct = ent(nc.semaphore("s_ct"))
        s_cv = ent(nc.semaphore("s_cv"))
        s_pc = ent(nc.semaphore("s_pc"))
        s_xl = [ent(nc.semaphore(f"s_xl{i}")) for i in range(2)]
        s_st = ent(nc.semaphore("s_st"))
        s_sc = ent(nc.semaphore("s_sc"))
        s_vx = ent(nc.semaphore("s_vx"))
        s_u = ent(nc.semaphore("s_u"))
        s_tp = ent(nc.semaphore("s_tp"))
        s_ue = ent(nc.semaphore("s_ue"))
        s_wl = [ent(nc.semaphore(f"s_wl{i}")) for i in range(2)]
        s_wq = ent(nc.semaphore("s_wq"))
        s_wq2 = ent(nc.semaphore("s_wq2"))
        s_gp = ent(nc.semaphore("s_gp"))
        s_up2 = ent(nc.semaphore("s_up2"))
        s_sg = ent(nc.semaphore("s_sg"))
        s_ep = ent(nc.semaphore("s_ep"))
        s_dl = [ent(nc.semaphore(f"s_dl{i}")) for i in range(2)]
        s_dc = ent(nc.semaphore("s_dc"))
        s_dp = ent(nc.semaphore("s_dp"))
        s_df = ent(nc.semaphore("s_df"))
        s_dt = ent(nc.semaphore("s_dt"))
        s_xv = ent(nc.semaphore("s_xv"))
        s_so = [ent(nc.semaphore(f"s_so{i}")) for i in range(2)]
        blk = ent(nc.Block())

        @blk.sync
        def _(q):
            if prefill:
                q.dma_start(out=x_out[:, :], in_=x_in[:, :]).then_inc(s_pc, 16)
            q.dma_start(out=cidf[:], in_=id_in[:]).then_inc(s_cid, 16)
            q.dma_start(out=sqp[0:64, :], in_=sg_in[:]).then_inc(s_cq, 16)
            q.wait_ge(s_ct, 1)
            q.dma_start(out=sqp[0:64, :], in_=su_in[:]).then_inc(s_cq, 16)
            q.wait_ge(s_ct, 2)
            q.dma_start(out=sqp[0:16, :], in_=sd_in[:]).then_inc(s_cq, 16)
            for t in range(NT):
                if t >= 2:
                    q.wait_ge(s_u, t - 1)
                q.dma_start(out=xt[:, t % 2, :],
                            in_=x_in[t * 128:(t + 1) * 128, :]
                            ).then_inc(s_xl[t % 2], 16)
            if prefill:
                q.wait_ge(s_pc, 16)
            for h in range(NH):
                for g in range(NG):
                    gi = h * NG + g
                    if gi >= 2:
                        q.wait_ge(s_wq, GRP * (gi - 1))
                        q.wait_ge(s_wq2, GRP * (gi - 1))
                    for m, w in enumerate((wg8, wu8)):
                        q.dma_start(out=wgu8[:, gi % 2, m, :, :],
                                    in_=w[g, :, :, :]).then_inc(s_wl[gi % 2], 16)
                for dm in range(16):
                    D = h * 16 + dm
                    if D >= 2:
                        q.wait_ge(s_dc, D - 1)
                    q.dma_start(out=wd8s[:, D % 2, :, :],
                                in_=wd8[dm, :, :, :]).then_inc(s_dl[D % 2], 16)
            q.wait_ge(s_so[0], NH * 32 * 16)
            q.wait_ge(s_so[1], NH * 32 * 16)

        @blk.scalar
        def _(s):
            n_sc = 0
            for t in range(NT):
                s.wait_ge(s_xl[t % 2], 16 * (t // 2 + 1))
                if t >= 1:
                    s.wait_ge(s_u, t)
                    s.wait_ge(s_tp, 16 * t)
                xa = xt[:, t % 2, :]
                n_sc = emit_ln_stats(s, xa, u_tm[:], cst, LNEPS, s_sc, n_sc)
                s.nop().then_inc(s_st, 1)
            def cast_gate(Qc):
                gi = Qc // GRP
                cc = Qc % GRP
                s.wait_ge(s_wl[gi % 2], 32 * (gi // 2 + 1))
                if Qc >= 2:
                    s.wait_ge(s_up2, Qc - 1)
                s.activation(wbuf[:, Qc % 2, 0, :, :],
                             wgu8[:, gi % 2, 0, :, cc * 128:(cc + 1) * 128],
                             AF.Copy).then_inc(s_wq, 1)
            for h in range(NH):
                cast_gate(h * NQ)
                cast_gate(h * NQ + 1)
                for qh in range(NQ):
                    Q = h * NQ + qh
                    if qh + 2 < NQ:
                        cast_gate(Q + 2)
                    s.wait_ge(s_gp, Q + 1)
                    if Q >= 2:
                        s.wait_ge(s_ep, Q - 1)
                    s.activation(sgv[:, Q % 2, :], pg[:, qh % 2, :], AF.Silu,
                                 scale=scol[:, qh:qh + 1]).then_inc(s_sg, 1)
                for dm in range(16):
                    D = h * 16 + dm
                    s.wait_ge(s_dl[D % 2], 16 * (D // 2 + 1))
                    if D >= 2:
                        s.wait_ge(s_dp, D - 1)
                    s.activation(wdb[:, D % 2, :, :], wd8s[:, D % 2, :, :],
                                 AF.Copy).then_inc(s_dc, 1)

        @blk.tensor
        def _(t):
            t.wait_ge(s_cid, 16)
            t.wait_ge(s_cq, 16)
            t.wait_ge(s_cv, 1)
            t.transpose(pt[:, 0, 0:128].bitcast(F32), sqp[0:64, :],
                        cidf[0:64, 0:64]).then_inc(s_ct, 1)
            t.wait_ge(s_cq, 32)
            t.wait_ge(s_cv, 2)
            t.transpose(pt[:, 0, 0:128].bitcast(F32), sqp[0:64, :],
                        cidf[0:64, 0:64]).then_inc(s_ct, 1)
            t.wait_ge(s_cq, 48)
            t.wait_ge(s_cv, 3)
            t.transpose(pt[:, 0, 0:32].bitcast(F32), sqp[0:16, :],
                        cidf[0:16, 0:16]).then_inc(s_ct, 1)
            t.wait_ge(s_cv, 4)
            for tt_ in range(NT):
                t.wait_ge(s_u, tt_ + 1)
                for c in range(NK):
                    g = tt_ * NK + c
                    if g >= 2:
                        t.wait_ge(s_ue, g - 1)
                    t.transpose(pt[:, g % 2, 0:128], u_tm[:, c * 128:(c + 1) * 128],
                                idb[:]).then_inc(s_tp, 1)
            for h in range(NH):
                t.wait_ge(s_ue, NK * 4 * (h + 1))
                for qh in range(NQ):
                    Q = h * NQ + qh
                    if Q >= 2:
                        t.wait_ge(s_ep, Q - 1)
                    for m, ps in ((0, pg), (1, pu)):
                        t.wait_ge(s_wq if m == 0 else s_wq2, Q + 1)
                        for k in range(NK):
                            ins = t.matmul(ps[:, qh % 2, :],
                                           wbuf[:, Q % 2, m, k, :],
                                           u_fm[:, k, h * 512:(h + 1) * 512],
                                           start=(k == 0), stop=(k == NK - 1))
                        if m == 0:
                            ins.then_inc(s_gp, 1)
                        else:
                            ins.then_inc(s_up2, 1)
                for dm in range(16):
                    D = h * 16 + dm
                    t.wait_ge(s_dc, D + 1)
                    if dm == 0:
                        t.wait_ge(s_ep, (h + 1) * NQ)
                    if D >= 2:
                        t.wait_ge(s_df, D - 1)
                    for qc in range(NQ):
                        ins = t.matmul(pd[:, D % 2, :], wdb[:, D % 2, qc, :],
                                       interm[:, qc, :],
                                       start=(qc == 0), stop=(qc == NQ - 1))
                    ins.then_inc(s_dp, 1)
                    if dm >= 1:
                        Dp = D - 1
                        t.wait_ge(s_df, Dp + 1)
                        for sub in range(4):
                            B = Dp * 4 + sub
                            if B >= 2:
                                t.wait_ge(s_xv, B - 1)
                            t.transpose(ptd[:, B % 2, 0:128],
                                        dfm[:, Dp % 2, sub * 128:(sub + 1) * 128],
                                        idb[:]).then_inc(s_dt, 1)
                D = h * 16 + 15
                t.wait_ge(s_df, D + 1)
                for sub in range(4):
                    B = D * 4 + sub
                    t.wait_ge(s_xv, B - 1)
                    t.transpose(ptd[:, B % 2, 0:128],
                                dfm[:, D % 2, sub * 128:(sub + 1) * 128],
                                idb[:]).then_inc(s_dt, 1)

        @blk.vector
        def _(v):
            v.wait_ge(s_cid, 16)
            v.tensor_copy(idb[:], cidf[:]).then_inc(s_cv, 1)
            v.wait_ge(s_ct, 1)
            v.tensor_copy(scol[:, 0:64], pt[:, 0, 0:128].bitcast(F32)).then_inc(s_cv, 1)
            v.wait_ge(s_ct, 2)
            v.tensor_copy(scol[:, 64:128], pt[:, 0, 0:128].bitcast(F32)).then_inc(s_cv, 1)
            v.wait_ge(s_ct, 3)
            v.tensor_copy(scol[:, 128:144],
                          pt[:, 0, 0:32].bitcast(F32)).then_inc(s_cv, 1)
            for tt_ in range(NT):
                v.wait_ge(s_st, tt_ + 1)
                if tt_ >= 1:
                    v.wait_ge(s_tp, 16 * tt_)
                v.scalar_tensor_tensor(
                    u_tm[:], xt[:, tt_ % 2, :], cst[:, 2:3],
                    cst[:, 5:6].broadcast_to([128, DM]),
                    ALU.subtract, ALU.mult).then_inc(s_u, 1)
                for c in range(NK):
                    g = tt_ * NK + c
                    v.wait_ge(s_tp, g + 1)
                    v.tensor_copy(u_fm[:, c, tt_ * 128:(tt_ + 1) * 128],
                                  pt[:, g % 2, 0:128]).then_inc(s_ue, 1)
            def cast_up(Qc):
                gi = Qc // GRP
                cc = Qc % GRP
                v.wait_ge(s_wl[gi % 2], 32 * (gi // 2 + 1))
                if Qc >= 2:
                    v.wait_ge(s_up2, Qc - 1)
                v.tensor_copy(wbuf[:, Qc % 2, 1, :, :],
                              wgu8[:, gi % 2, 1, :, cc * 128:(cc + 1) * 128]
                              ).then_inc(s_wq2, 1)
            for h in range(NH):
                if h == 1:
                    v.wait_ge(s_dp, 16)
                cast_up(h * NQ)
                cast_up(h * NQ + 1)
                for qh in range(NQ):
                    Q = h * NQ + qh
                    if qh + 2 < NQ:
                        cast_up(Q + 2)
                    v.wait_ge(s_up2, Q + 1)
                    v.tensor_scalar(upv[:, Q % 2, :], pu[:, qh % 2, :],
                                    scol[:, 64 + qh:64 + qh + 1], None,
                                    ALU.mult).then_inc(s_vx, 1)
                    v.wait_ge(s_sg, Q + 1)
                    v.wait_ge(s_vx, Q + 1)
                    v.tensor_tensor(interm[:, qh, :], sgv[:, Q % 2, :],
                                    upv[:, Q % 2, :], ALU.mult).then_inc(s_ep, 1)
                for dm in range(16):
                    D = h * 16 + dm
                    v.wait_ge(s_dp, D + 1)
                    if D >= 2:
                        v.wait_ge(s_dt, 4 * (D - 1))
                    v.tensor_scalar(dfm[:, D % 2, :], pd[:, D % 2, :],
                                    scol[:, 128 + dm:128 + dm + 1], None,
                                    ALU.mult).then_inc(s_df, 1)
                    if dm >= 1:
                        Dp = D - 1
                        for sub in range(4):
                            B = Dp * 4 + sub
                            v.wait_ge(s_dt, B + 1)
                            if B >= 2:
                                v.wait_ge(s_so[B % 2], 16 * (B // 2))
                            v.tensor_scalar(xev[:, B % 2, :], ptd[:, B % 2, 0:128],
                                            0.5, None, ALU.mult).then_inc(s_xv, 1)
                D = h * 16 + 15
                for sub in range(4):
                    B = D * 4 + sub
                    v.wait_ge(s_dt, B + 1)
                    v.wait_ge(s_so[B % 2], 16 * (B // 2))
                    v.tensor_scalar(xev[:, B % 2, :], ptd[:, B % 2, 0:128],
                                    0.5, None, ALU.mult).then_inc(s_xv, 1)

        @blk.gpsimd
        def _(g_):
            if prefill:
                g_.wait_ge(s_pc, 16)
            else:
                g_.wait_ge(s_u, NT)
            for h in range(NH):
                for dm in range(16):
                    D = h * 16 + dm
                    dmp = dm
                    for sub in range(4):
                        B = D * 4 + sub
                        g_.wait_ge(s_xv, B + 1)
                        g_.dma_start(
                            out=x_out[h * 512 + sub * 128:
                                      h * 512 + (sub + 1) * 128,
                                      dmp * 128:(dmp + 1) * 128],
                            in_=xev[:, B % 2, :],
                            accum_op=ALU.add).then_inc(s_so[B % 2], 16)


def build_ffn0():
    nc = bass.Bass("TRN2", target_bir_lowering=False)
    ext = {}
    ext["x_in"] = nc.dram_tensor("x_in", [TL, DM], F32, kind="ExternalInput").ap()
    ext["wg8"] = nc.dram_tensor("wg8", [32, 128, 16, 256], I8, kind="ExternalInput").ap()
    ext["wu8"] = nc.dram_tensor("wu8", [32, 128, 16, 256], I8, kind="ExternalInput").ap()
    ext["wd8"] = nc.dram_tensor("wd8", [16, 128, 64, 128], I8, kind="ExternalInput").ap()
    ext["sg_in"] = nc.dram_tensor("sg_in", [64, 128], F32, kind="ExternalInput").ap()
    ext["su_in"] = nc.dram_tensor("su_in", [64, 128], F32, kind="ExternalInput").ap()
    ext["sd_in"] = nc.dram_tensor("sd_in", [16, 128], F32, kind="ExternalInput").ap()
    ext["id_in"] = nc.dram_tensor("id_in", [128, 128], F32, kind="ExternalInput").ap()
    ext["x_out"] = nc.dram_tensor("x_out", [TL, DM], F32, kind="ExternalOutput").ap()
    emit_ffn0(nc, ext)
    return nc



def _ffn0_wlayout(wg8, wu8, wd8):
    wg = np.ascontiguousarray(wg8.reshape(16, 128, 32, 256).transpose(2, 1, 0, 3))
    wu = np.ascontiguousarray(wu8.reshape(16, 128, 32, 256).transpose(2, 1, 0, 3))
    wd = np.ascontiguousarray(wd8.reshape(64, 128, 16, 128).transpose(2, 1, 0, 3))
    return wg, wu, wd


def run_ffn0(x, wg, wu, wd, sg, su, sd):
    nc = _prog("ffn0", build_ffn0)
    ident = np.eye(128, dtype=np.float32)
    in_maps = []
    for c in range(NC):
        in_maps.append({
            "x_in": _f32(x[c * TL:(c + 1) * TL]),
            "wg8": wg, "wu8": wu, "wd8": wd,
            "sg_in": _f32(sg.reshape(64, 128)),
            "su_in": _f32(su.reshape(64, 128)),
            "sd_in": _f32(sd.reshape(16, 128)),
            "id_in": ident,
        })
    res = run_bass_kernel_spmd(nc, in_maps, core_ids=CORE_IDS)
    return np.concatenate([r["x_out"] for r in res.results], axis=0)




def emit_uprep(nc, ext, mamba, tag="", fused=False):
    sb = lambda n, s, d: nc.sbuf_tensor(tag + n, s, d)
    ps = lambda n, s, d: nc.psum_tensor(tag + n, s, d)
    x_in = ext["x_in"]
    id_in = ext["id_in"]
    lnw_in = ext.get("lnw_in")
    u_out = ext["u_out"]
    u_full = ext.get("u_full")
    core_sl = ext.get("core_sl")

    NT = TL // 128
    NK = DM // 128
    if mamba:
        e1, e2 = LNEPS, 1e-5
        a_, b_ = 1.0 + e2, e1 * e2
    else:
        a_, b_ = 1.0, LNEPS

    with ExitStack() as ctx:
        ent = ctx.enter_context
        xt = ent(sb("xt", [128, 2, DM], F32))
        u_tm = ent(sb("u_tm", [128, DM], BF16))
        uev = ent(sb("uev", [128, 4, 128], BF16))
        cidf = ent(sb("cidf", [128, 128], F32))
        idb = ent(sb("idb", [128, 128], BF16))
        scr = ent(sb("scr", [128, DM], F32))
        cst = ent(sb("cst", [128, 8], F32))
        wcol = ent(sb("wcol", [128, 16], F32))
        sqp = ent(sb("sqp", [128, 128], F32))
        pt = ent(ps("pt", [128, 2, 1024], BF16))

        s_cid = ent(nc.semaphore("s_cid"))
        s_cq = ent(nc.semaphore("s_cq"))
        s_ct = ent(nc.semaphore("s_ct"))
        s_cv = ent(nc.semaphore("s_cv"))
        s_xl = [ent(nc.semaphore(f"s_xl{i}")) for i in range(2)]
        s_st = ent(nc.semaphore("s_st"))
        s_sc = ent(nc.semaphore("s_sc"))
        s_u = ent(nc.semaphore("s_u"))
        s_tp = ent(nc.semaphore("s_tp"))
        s_ue = ent(nc.semaphore("s_ue"))
        s_so = [ent(nc.semaphore(f"s_so{i}")) for i in range(4)]
        s_ag = ent(nc.semaphore("s_ag"))
        blk = ent(nc.Block())

        NCONST = 2 if mamba else 1

        @blk.sync
        def _(q):
            q.dma_start(out=cidf[:], in_=id_in[:]).then_inc(s_cid, 16)
            if mamba:
                q.dma_start(out=sqp[0:16, :], in_=lnw_in[:]).then_inc(s_cq, 16)
            for t in range(NT):
                if t >= 2:
                    q.wait_ge(s_u, t - 1)
                q.dma_start(out=xt[:, t % 2, :],
                            in_=x_in[t * 128:(t + 1) * 128, :]
                            ).then_inc(s_xl[t % 2], 16)
            for i in range(4):
                q.wait_ge(s_so[i], 16 * (NT * NK // 4))

        @blk.scalar
        def _(s):
            n_sc = 0
            for t in range(NT):
                s.wait_ge(s_xl[t % 2], 16 * (t // 2 + 1))
                if t >= 1:
                    s.wait_ge(s_u, t)
                xa = xt[:, t % 2, :]
                n_sc = emit_ln_stats(s, xa, scr[:], cst, b_, s_sc, n_sc, a=a_)
                s.nop().then_inc(s_st, 1)

        @blk.tensor
        def _(t):
            t.wait_ge(s_cid, 16)
            t.wait_ge(s_cv, 1)
            if mamba:
                t.wait_ge(s_cq, 16)
                t.transpose(pt[:, 0, 0:32].bitcast(F32), sqp[0:16, :],
                            cidf[0:16, 0:16]).then_inc(s_ct, 1)
            t.wait_ge(s_cv, NCONST)
            for tt_ in range(NT):
                t.wait_ge(s_u, tt_ + 1)
                for c in range(NK):
                    g = tt_ * NK + c
                    if g >= 2:
                        t.wait_ge(s_ue, g - 1)
                    t.transpose(pt[:, g % 2, 0:128], u_tm[:, c * 128:(c + 1) * 128],
                                idb[:]).then_inc(s_tp, 1)

        @blk.vector
        def _(v):
            v.wait_ge(s_cid, 16)
            v.tensor_copy(idb[:], cidf[:]).then_inc(s_cv, 1)
            if mamba:
                v.wait_ge(s_ct, 1)
                v.tensor_copy(wcol[:, 0:16],
                              pt[:, 0, 0:32].bitcast(F32)).then_inc(s_cv, 1)
            for tt_ in range(NT):
                v.wait_ge(s_st, tt_ + 1)
                if tt_ >= 1:
                    v.wait_ge(s_tp, 16 * tt_)
                v.scalar_tensor_tensor(
                    u_tm[:], xt[:, tt_ % 2, :], cst[:, 2:3],
                    cst[:, 5:6].broadcast_to([128, DM]),
                    ALU.subtract, ALU.mult).then_inc(s_u, 1)
                for c in range(NK):
                    g = tt_ * NK + c
                    v.wait_ge(s_tp, g + 1)
                    if g >= 4:
                        v.wait_ge(s_so[g % 4], 16 * (g // 4 - 0))
                    if mamba:
                        v.tensor_scalar(uev[:, g % 4, :], pt[:, g % 2, 0:128],
                                        wcol[:, c:c + 1], None,
                                        ALU.mult).then_inc(s_ue, 1)
                    else:
                        v.tensor_copy(uev[:, g % 4, :],
                                      pt[:, g % 2, 0:128]).then_inc(s_ue, 1)

        @blk.gpsimd
        def _(g_):
            for tt_ in range(NT):
                for c in range(NK):
                    g = tt_ * NK + c
                    g_.wait_ge(s_ue, g + 1)
                    if fused:
                        dst = u_out[tt_ // 2, :, c,
                                    (tt_ % 2) * 128:(tt_ % 2) * 128 + 128]
                    else:
                        dst = u_out[c, :, tt_ * 128:(tt_ + 1) * 128]
                    g_.dma_start(out=dst,
                                 in_=uev[:, g % 4, :]).then_inc(s_so[g % 4], 16)
            if fused:
                for i in range(4):
                    g_.wait_ge(s_so[i], 16 * (NT * NK // 4))
                g_.collective_compute(
                    "AllGather", ALU.bypass, replica_groups=[CORE_IDS],
                    ins=[u_out.opt()], outs=[u_full.opt()]).then_inc(s_ag, 1)
                g_.wait_ge(s_ag, 1)


def build_uprep(mamba):
    nc = bass.Bass("TRN2", target_bir_lowering=False)
    ext = {}
    ext["x_in"] = nc.dram_tensor("x_in", [TL, DM], F32, kind="ExternalInput").ap()
    ext["id_in"] = nc.dram_tensor("id_in", [128, 128], F32, kind="ExternalInput").ap()
    if mamba:
        ext["lnw_in"] = nc.dram_tensor("lnw_in", [16, 128], F32, kind="ExternalInput").ap()
    ext["u_out"] = nc.dram_tensor("u_out", [16, 128, TL], BF16, kind="ExternalOutput").ap()
    emit_uprep(nc, ext, mamba)
    return nc



def run_uprep(x, mamba, lnw=None):
    nc = _prog("uprep_m" if mamba else "uprep_p",
               lambda: build_uprep(mamba))
    ident = np.eye(128, dtype=np.float32)
    in_maps = []
    for c in range(NC):
        m = {"x_in": _f32(x[c * TL:(c + 1) * TL]), "id_in": ident}
        if mamba:
            m["lnw_in"] = _f32(lnw.reshape(16, 128))
        in_maps.append(m)
    res = run_bass_kernel_spmd(nc, in_maps, core_ids=CORE_IDS)
    return np.concatenate([r["u_out"] for r in res.results], axis=2)



def build_tail(mamba):
    KDIM = 4096 if mamba else 2048
    NKC = KDIM // 128
    nc = bass.Bass("TRN2", target_bir_lowering=False)
    y_in = nc.dram_tensor("y_in", [TL, KDIM], BF16, kind="ExternalInput").ap()
    x_in = nc.dram_tensor("x_in", [TL, DM], F32, kind="ExternalInput").ap()
    w_in = nc.dram_tensor("w_in", [16, 128, NKC, 128], F32, kind="ExternalInput").ap()
    id_in = nc.dram_tensor("id_in", [128, 128], F32, kind="ExternalInput").ap()
    if mamba:
        nw_in = nc.dram_tensor("nw_in", [32, 128], F32, kind="ExternalInput").ap()
    x_out = nc.dram_tensor("x_out", [TL, DM], F32, kind="ExternalOutput").ap()

    NT = TL // 128
    RMSEPS = 1e-5

    with ExitStack() as ctx:
        ent = ctx.enter_context
        yt = ent(nc.sbuf_tensor("yt", [128, 2, KDIM], BF16))
        yr = ent(nc.sbuf_tensor("yr", [128, 2, KDIM], BF16))
        y_fm = ent(nc.sbuf_tensor("y_fm", [128, NKC, TL], BF16))
        w8s = ent(nc.sbuf_tensor("w8s", [128, 2, NKC, 128], F32))
        wb = ent(nc.sbuf_tensor("wb", [128, 2, NKC, 128], BF16))
        xev = ent(nc.sbuf_tensor("xev", [128, 2, 128], F32))
        xt = ent(nc.sbuf_tensor("xt", [128, 2, DM], F32))
        hev = ent(nc.sbuf_tensor("hev", [128, 2, DM], F32))
        cidf = ent(nc.sbuf_tensor("cidf", [128, 128], F32))
        idb = ent(nc.sbuf_tensor("idb", [128, 128], BF16))
        scr = ent(nc.sbuf_tensor("scr", [128, KDIM], F32))
        cst = ent(nc.sbuf_tensor("cst", [128, 8], F32))
        cstx = ent(nc.sbuf_tensor("cstx", [128, 8], F32))
        wcol = ent(nc.sbuf_tensor("wcol", [128, 32], F32))
        sqp = ent(nc.sbuf_tensor("sqp", [128, 128], F32))

        pt = ent(nc.psum_tensor("pt", [128, 2, 1024], BF16))
        po = ent(nc.psum_tensor("po", [128, 2, 512], F32))

        s_cid = ent(nc.semaphore("s_cid"))
        s_cq = ent(nc.semaphore("s_cq"))
        s_ct = ent(nc.semaphore("s_ct"))
        s_cv = ent(nc.semaphore("s_cv"))
        s_pc = ent(nc.semaphore("s_pc"))
        s_yl = [ent(nc.semaphore(f"s_yl{i}")) for i in range(2)]
        s_xl = [ent(nc.semaphore(f"s_xl{i}")) for i in range(2)]
        s_st = ent(nc.semaphore("s_st"))
        s_sc = ent(nc.semaphore("s_sc"))
        s_sx = ent(nc.semaphore("s_sx"))
        s_yr = ent(nc.semaphore("s_yr"))
        s_tp = ent(nc.semaphore("s_tp"))
        s_ue = ent(nc.semaphore("s_ue"))
        s_wl = [ent(nc.semaphore(f"s_wl{i}")) for i in range(2)]
        s_wb = ent(nc.semaphore("s_wb"))
        s_mm = ent(nc.semaphore("s_mm"))
        s_df = ent(nc.semaphore("s_df"))
        s_dt = ent(nc.semaphore("s_dt"))
        s_xv = ent(nc.semaphore("s_xv"))
        s_hv = ent(nc.semaphore("s_hv"))
        s_so = [ent(nc.semaphore(f"s_so{i}")) for i in range(2)]
        s_ho = [ent(nc.semaphore(f"s_ho{i}")) for i in range(2)]
        blk = ent(nc.Block())

        dfm = yr[:, :, 0:512]
        NCONST = 2 if mamba else 1

        @blk.sync
        def _(q):
            q.dma_start(out=x_out[:, :], in_=x_in[:, :]).then_inc(s_pc, 16)
            q.dma_start(out=cidf[:], in_=id_in[:]).then_inc(s_cid, 16)
            if mamba:
                q.dma_start(out=sqp[0:32, :], in_=nw_in[0:32, :]).then_inc(s_cq, 16)
            for t in range(NT):
                if t >= 2:
                    if mamba:
                        q.wait_ge(s_yr, t - 1)
                    else:
                        q.wait_ge(s_tp, NKC * (t - 1))
                q.dma_start(out=yt[:, t % 2, :],
                            in_=y_in[t * 128:(t + 1) * 128, :]
                            ).then_inc(s_yl[t % 2], 16)
            if mamba:
                for t in range(NT):
                    if t >= 2:
                        q.wait_ge(s_hv, t - 1)
                    q.dma_start(out=xt[:, t % 2, :],
                                in_=x_in[t * 128:(t + 1) * 128, :]
                                ).then_inc(s_xl[t % 2], 16)
            for ot in range(16):
                if ot >= 2:
                    q.wait_ge(s_wb, ot - 1)
                q.dma_start(out=w8s[:, ot % 2, :, :],
                            in_=w_in[ot, :, :, :]).then_inc(s_wl[ot % 2], 16)
            for i in range(2):
                q.wait_ge(s_so[i], 16 * 32)
            if mamba:
                for i in range(2):
                    q.wait_ge(s_ho[i], 16 * 4)

        @blk.scalar
        def _(s):
            n_sc = 0
            for t in range(NT):
                s.wait_ge(s_yl[t % 2], 16 * (t // 2 + 1))
                if mamba:
                    if t >= 1:
                        s.wait_ge(s_yr, t)
                    ya = yt[:, t % 2, :]
                    def stp(ins):
                        nonlocal n_sc
                        ins.then_inc(s_sc, 1); n_sc += 1
                        s.wait_ge(s_sc, n_sc)
                    stp(s.activation(scr[:], ya, AF.Square, accum_out=cst[:, 0:1]))
                    stp(s.activation(cst[:, 3:4], cst[:, 0:1], AF.Copy,
                                     scale=0.0, bias=RMSEPS))
                    stp(s.activation(cst[:, 1:2], cst[:, 0:1], AF.Ln,
                                     scale=1.0 / KDIM, bias=cst[:, 3:4]))
                    stp(s.activation(cst[:, 2:3], cst[:, 1:2], AF.Exp, scale=-0.5))
                s.nop().then_inc(s_st, 1)
            if mamba:
                s.wait_ge(s_sc, n_sc)
                n2 = 0
                for t in range(NT):
                    s.wait_ge(s_xl[t % 2], 16 * (t // 2 + 1))
                    if t >= 1:
                        s.wait_ge(s_hv, t)
                    n2 = emit_ln_stats(s, xt[:, t % 2, :], scr[:, 0:DM], cstx,
                                       LNEPS, s_sx, n2)
                    s.nop().then_inc(s_sx, 1)
                    n2 += 1

        @blk.tensor
        def _(t):
            t.wait_ge(s_cid, 16)
            t.wait_ge(s_cv, 1)
            if mamba:
                t.wait_ge(s_cq, 16)
                t.transpose(pt[:, 0, 0:64].bitcast(F32), sqp[0:32, :],
                            cidf[0:32, 0:32]).then_inc(s_ct, 1)
            t.wait_ge(s_cv, NCONST)
            for tt_ in range(NT):
                if mamba:
                    t.wait_ge(s_yr, tt_ + 1)
                else:
                    t.wait_ge(s_yl[tt_ % 2], 16 * (tt_ // 2 + 1))
                for c in range(NKC):
                    g = tt_ * NKC + c
                    if g >= 2:
                        t.wait_ge(s_ue, g - 1)
                    src = yr[:, tt_ % 2, c * 128:(c + 1) * 128] if mamba else \
                        yt[:, tt_ % 2, c * 128:(c + 1) * 128]
                    t.transpose(pt[:, g % 2, 0:128], src, idb[:]).then_inc(s_tp, 1)
            t.wait_ge(s_ue, NT * NKC)
            for ot in range(16):
                t.wait_ge(s_wb, ot + 1)
                for half in range(2):
                    J = ot * 2 + half
                    if J >= 2:
                        t.wait_ge(s_df, J - 1)
                    for kc in range(NKC):
                        ins = t.matmul(po[:, J % 2, :], wb[:, ot % 2, kc, :],
                                       y_fm[:, kc, half * 512:(half + 1) * 512],
                                       start=(kc == 0), stop=(kc == NKC - 1))
                    ins.then_inc(s_mm, 1)
                    if J >= 1:
                        Jp = J - 1
                        t.wait_ge(s_df, Jp + 1)
                        for sub in range(4):
                            B = Jp * 4 + sub
                            if B >= 2:
                                t.wait_ge(s_xv, B - 1)
                            t.transpose(pt[:, B % 2, 0:128],
                                        dfm[:, Jp % 2, sub * 128:(sub + 1) * 128],
                                        idb[:]).then_inc(s_dt, 1)
            Jp = 31
            t.wait_ge(s_df, 32)
            for sub in range(4):
                B = Jp * 4 + sub
                t.wait_ge(s_xv, B - 1)
                t.transpose(pt[:, B % 2, 0:128],
                            dfm[:, Jp % 2, sub * 128:(sub + 1) * 128],
                            idb[:]).then_inc(s_dt, 1)

        @blk.vector
        def _(v):
            v.wait_ge(s_cid, 16)
            v.tensor_copy(idb[:], cidf[:]).then_inc(s_cv, 1)
            if mamba:
                v.wait_ge(s_ct, 1)
                v.tensor_copy(wcol[:, 0:32],
                              pt[:, 0, 0:64].bitcast(F32)).then_inc(s_cv, 1)
            for tt_ in range(NT):
                v.wait_ge(s_st, tt_ + 1)
                if mamba:
                    if tt_ >= 1:
                        v.wait_ge(s_tp, NKC * tt_)
                    v.tensor_scalar(yr[:, tt_ % 2, :], yt[:, tt_ % 2, :],
                                    cst[:, 2:3], None, ALU.mult).then_inc(s_yr, 1)
                for c in range(NKC):
                    g = tt_ * NKC + c
                    v.wait_ge(s_tp, g + 1)
                    if mamba:
                        v.tensor_scalar(y_fm[:, c, tt_ * 128:(tt_ + 1) * 128],
                                        pt[:, g % 2, 0:128],
                                        wcol[:, c:c + 1], None,
                                        ALU.mult).then_inc(s_ue, 1)
                    else:
                        v.tensor_copy(y_fm[:, c, tt_ * 128:(tt_ + 1) * 128],
                                      pt[:, g % 2, 0:128]).then_inc(s_ue, 1)
            if mamba:
                for tt_ in range(NT):
                    v.wait_ge(s_sx, 8 * (tt_ + 1))
                    if tt_ >= 2:
                        v.wait_ge(s_ho[tt_ % 2], 16 * (tt_ // 2))
                    v.scalar_tensor_tensor(
                        hev[:, tt_ % 2, :], xt[:, tt_ % 2, :], cstx[:, 2:3],
                        cstx[:, 5:6].broadcast_to([128, DM]),
                        ALU.subtract, ALU.mult).then_inc(s_hv, 1)
            for it in range(18):
                if it >= 2:
                    for half in range(2):
                        J = (it - 2) * 2 + half
                        v.wait_ge(s_mm, J + 1)
                        if J >= 2:
                            v.wait_ge(s_dt, 4 * (J - 1))
                        v.tensor_copy(dfm[:, J % 2, :],
                                      po[:, J % 2, :]).then_inc(s_df, 1)
                        if J >= 1:
                            Jp = J - 1
                            for sub in range(4):
                                B = Jp * 4 + sub
                                v.wait_ge(s_dt, B + 1)
                                if B >= 2:
                                    v.wait_ge(s_so[B % 2], 16 * (B // 2))
                                v.tensor_copy(xev[:, B % 2, :],
                                              pt[:, B % 2, 0:128]).then_inc(s_xv, 1)
                if it < 16:
                    ot = it
                    v.wait_ge(s_wl[ot % 2], 16 * (ot // 2 + 1))
                    if ot >= 2:
                        v.wait_ge(s_mm, 2 * (ot - 1))
                    v.tensor_copy(wb[:, ot % 2, :, :],
                                  w8s[:, ot % 2, :, :]).then_inc(s_wb, 1)
            Jp = 31
            for sub in range(4):
                B = Jp * 4 + sub
                v.wait_ge(s_dt, B + 1)
                v.wait_ge(s_so[B % 2], 16 * (B // 2))
                v.tensor_copy(xev[:, B % 2, :],
                              pt[:, B % 2, 0:128]).then_inc(s_xv, 1)

        @blk.gpsimd
        def _(g_):
            g_.wait_ge(s_pc, 16)
            if mamba:
                for tt_ in range(NT):
                    g_.wait_ge(s_hv, tt_ + 1)
                    g_.dma_start(out=x_out[tt_ * 128:(tt_ + 1) * 128, :],
                                 in_=hev[:, tt_ % 2, :],
                                 accum_op=ALU.add).then_inc(s_ho[tt_ % 2], 16)
            for ot in range(16):
                for half in range(2):
                    Jp = ot * 2 + half
                    for sub in range(4):
                        B = Jp * 4 + sub
                        g_.wait_ge(s_xv, B + 1)
                        g_.dma_start(
                            out=x_out[half * 512 + sub * 128:
                                      half * 512 + (sub + 1) * 128,
                                      ot * 128:(ot + 1) * 128],
                            in_=xev[:, B % 2, :],
                            accum_op=ALU.add).then_inc(s_so[B % 2], 16)

    return nc



def _tail_wlayout(W):
    KDIM = W.shape[1]
    NKC = KDIM // 128
    wt = _f32(W).T.reshape(NKC, 128, 16, 128).transpose(2, 1, 0, 3)
    return np.ascontiguousarray(wt)


def run_tail(x, y_full, w_t, mamba, nw=None):
    nc = _prog("tail_m" if mamba else "tail_p", lambda: build_tail(mamba))
    ident = np.eye(128, dtype=np.float32)
    in_maps = []
    for c in range(NC):
        m = {"x_in": _f32(x[c * TL:(c + 1) * TL]),
             "y_in": np.ascontiguousarray(y_full[c * TL:(c + 1) * TL]),
             "w_in": w_t, "id_in": ident}
        if mamba:
            m["nw_in"] = _f32(nw.reshape(32, 128))
        in_maps.append(m)
    res = run_bass_kernel_spmd(nc, in_maps, core_ids=CORE_IDS)
    return np.concatenate([r["x_out"] for r in res.results], axis=0)




def emit_b2(nc, ext, tag=""):
    NCH = 64
    NUC = 32
    sb = lambda n, s, d: nc.sbuf_tensor(tag + n, s, d)
    ps = lambda n, s, d: nc.psum_tensor(tag + n, s, d)
    u_in = ext["u_in"]
    wq_in = ext["wq_in"]
    wk_in = ext["wk_in"]
    wv_in = ext["wv_in"]
    wg_in = ext["wg_in"]
    wgk_in = ext["wgk_in"]
    cos_in = ext["cos_in"]
    sin_in = ext["sin_in"]
    swp_in = ext["swp_in"]
    tri_in = ext["tri_in"]
    suf_in = ext["suf_in"]
    one_in = ext["one_in"]
    msk_in = ext["msk_in"]
    id_in = ext["id_in"]
    id2_in = ext["id2_in"]
    o_out = ext["o_out"]
    g_dram = ext["g_dram"]

    with ExitStack() as ctx:
        ent = ctx.enter_context
        us = ent(sb("us", [128, 2, 16, 256], BF16))
        q_fm = ent(sb("q_fm", [128, 2, T], BF16))
        k_fm = ent(sb("k_fm", [128, 2, T], BF16))
        v_fm = ent(sb("v_fm", [128, 2, T], BF16))
        gk_tm = ent(sb("gk_tm", [128, NCH, 4], F32))
        gkr = ent(sb("gkr", [128, NCH, 4], F32))
        GS = ent(sb("GS", [128, NCH, 8], F32))
        dsum = ent(sb("dsum", [128, NCH, 4], F32))
        dex_all = ent(sb("dex_all", [128, NCH, 4], F32))
        eG_all = ent(sb("eG_all", [128, NCH, 4], F32))
        eS_all = ent(sb("eS_all", [128, NCH, 4], F32))
        lg1 = ent(sb("lg1", [128, NCH * 4], F32))
        lg2 = ent(sb("lg2", [128, NCH * 4], F32))
        wqb = ent(sb("wqb", [128, 2, 16, 128], BF16))
        wkb = ent(sb("wkb", [128, 2, 16, 128], BF16))
        wvb = ent(sb("wvb", [128, 2, 16, 128], BF16))
        wgb = ent(sb("wgb", [128, 16, 256], BF16))
        wgkb = ent(sb("wgkb", [128, 16, 4], BF16))
        wgks = ent(sb("wgks", [128, 16, 4], F32))
        wstg = ent(sb("wstg", [128, 2, 2048], F32))
        csb = ent(sb("csb", [128, 2, 256], BF16))
        snb = ent(sb("snb", [128, 2, 256], BF16))
        qraw = ent(sb("qraw", [128, 2, 256], BF16))
        t1b = ent(sb("t1b", [128, 2, 256], BF16))
        t2b = ent(sb("t2b", [128, 2, 256], BF16))
        gev = ent(sb("gev", [128, 2, 256], BF16))
        tri = ent(sb("tri", [128, 128], F32))
        suf = ent(sb("suf", [128, 128], F32))
        onesf = ent(sb("onesf", [128, 128], F32))
        mska = ent(sb("mska", [128, 128], F32))
        swpb = ent(sb("swpb", [128, 128], BF16))
        cidf = ent(sb("cidf", [128, 128], F32))
        idb = ent(sb("idb", [128, 128], BF16))
        id2f = ent(sb("id2f", [128, 64], F32))
        id2b = ent(sb("id2b", [128, 64], BF16))
        rsD = ent(sb("rsD", [128, 2, 128], F32))
        dlg = ent(sb("dlg", [128, 2, 128], F32))
        dexb = ent(sb("dexb", [128, 2, 128], BF16))
        attb = ent(sb("attb", [128, 2, 128], BF16))
        ktb = ent(sb("ktb", [128, 2, 4, 64], BF16))
        vtb = ent(sb("vtb", [128, 2, 4, 64], BF16))
        vdb = ent(sb("vdb", [128, 2, 4, 64], BF16))
        P_sb = ent(sb("P_sb", [128, 2, 64], F32))
        P_bf = ent(sb("P_bf", [128, 2, 64], BF16))
        o_sb = ent(sb("o_sb", [128, 2, 4, 64], F32))
        ob1 = ent(sb("ob1", [128, 256], F32))
        grl = ent(sb("grl", [128, 2, 256], BF16))
        sgb = ent(sb("sgb", [128, 2, 256], BF16))
        oob = ent(sb("oob", [128, 2, 256], BF16))
        osc = ent(sb("osc", [128, 256], F32))
        nrm = ent(sb("nrm", [128, 2, 12], F32))
        epsc = ent(sb("epsc", [128, 1], F32))

        ps_a = ent(ps("ps_a", [128, 2, 512], F32))
        ps_b = ent(ps("ps_b", [128, 2, 512], F32))
        ps_c = ent(ps("ps_c", [128, 2, 512], F32))
        ps_d = ent(ps("ps_d", [128, 2, 512], F32))
        qsw = ps_c
        pT = ps_d.ap().bitcast(BF16)

        sem = {}
        for name in ["s_cn", "s_wst0", "s_wst1", "s_wrdy", "s_ul0", "s_ul1",
                     "s_cs0", "s_cs1", "s_prj", "s_pev", "s_qsw", "s_rt1",
                     "s_rp1", "s_rp2", "s_gmm", "s_gev", "s_gst0", "s_gst1",
                     "s_lgv", "s_lgs", "s_lgd", "s_gsm", "s_gse", "s_gsv",
                     "s_bat", "s_bx", "s_rsd", "s_gib", "s_dlg", "s_dex",
                     "s_sct", "s_att", "s_trp", "s_kv1", "s_kv2", "s_oim",
                     "s_oev", "s_smm", "s_pst", "s_gt1", "s_rst", "s_sgl",
                     "s_grl0", "s_grl1", "s_oo", "s_ost0", "s_ost1"]:
            sem[name] = ent(nc.semaphore(name))
        sem["s_pbf"] = ent(nc.semaphore("s_pbf"))
        sem["s_ob1"] = ent(nc.semaphore("s_ob1"))
        S = lambda n: sem[n]
        blk = ent(nc.Block())

        wblocks = []
        for ot in range(2):
            wblocks.append((wqb, ot, wq_in[ot]))
            wblocks.append((wkb, ot, wk_in[ot]))
            wblocks.append((wvb, ot, wv_in[ot]))
        NW = len(wblocks)

        @blk.sync
        def _(q):
            for t_i, (dst, ap_) in enumerate((
                    (cidf, id_in), (tri, tri_in), (suf, suf_in),
                    (onesf, one_in), (mska, msk_in))):
                q.dma_start(out=dst[:], in_=ap_[:]).then_inc(S("s_cn"), 16)
            q.dma_start(out=swpb[:], in_=swp_in[:]).then_inc(S("s_cn"), 16)
            q.dma_start(out=id2f[:], in_=id2_in[:]).then_inc(S("s_cn"), 16)
            q.dma_start(out=wgks[:], in_=wgk_in[:]).then_inc(S("s_cn"), 16)
            for b in range(NW + 2):
                if b >= 2:
                    q.wait_ge(S("s_wrdy"), b - 1)
                if b < NW:
                    q.dma_start(out=wstg[:, b % 2, :].rearrange(
                                    "p (kc c) -> p kc c", c=128),
                                in_=wblocks[b][2].rearrange("kc p c -> p kc c")
                                ).then_inc(S(f"s_wst{b % 2}"), 16)
                else:
                    half = b - NW
                    q.dma_start(out=wstg[:, b % 2, :].rearrange(
                                    "p (kc c) -> p kc c", c=256),
                                in_=wg_in[half * 8:(half + 1) * 8]
                                .rearrange("kc p c -> p kc c")
                                ).then_inc(S(f"s_wst{b % 2}"), 16)
            for uc in range(NUC):
                if uc >= 2:
                    q.wait_ge(S("s_gmm"), 2 * (uc - 1))
                    q.wait_ge(S("s_rt1"), 4 * (uc - 1))
                    q.wait_ge(S("s_rp1"), 4 * (uc - 1))
                q.dma_start(out=us[:, uc % 2, :, :],
                            in_=u_in[uc]).then_inc(S(f"s_ul{uc % 2}"), 16)
                q.dma_start(out=csb[:, uc % 2, :],
                            in_=cos_in[uc]).then_inc(S(f"s_cs{uc % 2}"), 16)
                q.dma_start(out=snb[:, uc % 2, :],
                            in_=sin_in[uc]).then_inc(S(f"s_cs{uc % 2}"), 16)
            for c in range(NCH):
                if c >= 2:
                    q.wait_ge(S("s_sgl"), c - 1)
                q.wait_ge(S(f"s_gst{c % 2}"), 16 * (c // 2 + 1))
                q.dma_start(out=grl[:, c % 2, :],
                            in_=g_dram[c]).then_inc(S(f"s_grl{c % 2}"), 16)
            for i in range(2):
                q.wait_ge(S(f"s_ost{i}"), 16 * (NCH // 2))

        @blk.scalar
        def _(s):
            dsts = [(wb_, ot) for (wb_, ot, _) in wblocks]
            for b in range(NW + 2):
                s.wait_ge(S(f"s_wst{b % 2}"), 16 * (b // 2 + 1))
                if b < NW:
                    wb_, ot = dsts[b]
                    s.activation(wb_[:, ot, :, :], wstg[:, b % 2, :],
                                 AF.Copy).then_inc(S("s_wrdy"), 1)
                else:
                    half = b - NW
                    s.activation(wgb[:, half * 8:(half + 1) * 8, :],
                                 wstg[:, b % 2, :],
                                 AF.Copy).then_inc(S("s_wrdy"), 1)
            s.wait_ge(S("s_lgv"), 3)
            s.activation(lg2[:], lg2[:], AF.Exp, scale=-1.0).then_inc(S("s_lgs"), 1)
            s.wait_ge(S("s_lgs"), 1)
            s.activation(lg2[:], lg2[:], AF.Ln, bias=1.0).then_inc(S("s_lgs"), 1)
            s.wait_ge(S("s_bat"), 2)
            s.activation(eG_all[:].opt(), GS[:, :, 0:4], AF.Exp).then_inc(S("s_bx"), 1)
            s.activation(eS_all[:].opt(), GS[:, :, 4:8], AF.Exp).then_inc(S("s_bx"), 1)
            s.activation(dex_all[:].opt(), dsum[:].opt(), AF.Exp).then_inc(S("s_bx"), 1)
            for c in range(NCH):
                for h in range(4):
                    u = c * 4 + h
                    s.wait_ge(S("s_dlg"), u + 1)
                    if u >= 2:
                        s.wait_ge(S("s_att"), u - 1)
                    s.activation(dexb[:, u % 2, :], dlg[:, u % 2, :],
                                 AF.Exp).then_inc(S("s_dex"), 1)
                    s.wait_ge(S("s_trp"), u + 1)
                    if c >= 2:
                        s.wait_ge(S("s_smm"), 2 * (c - 1))
                    s.activation(ktb[:, c % 2, h, :], pT[:, u % 2, 0:64],
                                 AF.Copy, scale=0.125).then_inc(S("s_kv2"), 1)
                s.wait_ge(S("s_gt1"), 2 * (c + 1))
                s.activation(nrm[:, c % 2, 8:12], nrm[:, c % 2, 0:4], AF.Ln,
                             scale=1.0 / 64, bias=epsc[:]).then_inc(S("s_rst"), 1)
                s.wait_ge(S("s_rst"), 2 * c + 1)
                s.activation(nrm[:, c % 2, 8:12], nrm[:, c % 2, 8:12], AF.Exp,
                             scale=-0.5).then_inc(S("s_rst"), 1)
                s.wait_ge(S(f"s_grl{c % 2}"), 16 * (c // 2 + 1))
                if c >= 2:
                    s.wait_ge(S("s_oo"), c - 1)
                s.activation(sgb[:, c % 2, :], grl[:, c % 2, :],
                             AF.Silu).then_inc(S("s_sgl"), 1)

        @blk.tensor
        def _(t):
            t.wait_ge(S("s_cn"), 8 * 16)
            t.wait_ge(S("s_wrdy"), NW + 2)
            t.wait_ge(S("s_gsv"), 1)
            for uc in range(NUC):
                t.wait_ge(S(f"s_ul{uc % 2}"), 16 * (uc // 2 + 1))
                for ot in range(2):
                    for w in range(3):
                        gidx = uc * 6 + ot * 3 + w
                        if gidx >= 2:
                            t.wait_ge(S("s_pev"), gidx - 1)
                        wb_ = (wqb, wkb, wvb)[w]
                        for kc in range(16):
                            ins = t.matmul(ps_a[:, gidx % 2, 0:256],
                                           wb_[:, ot, kc, :],
                                           us[:, uc % 2, kc, :],
                                           start=(kc == 0), stop=(kc == 15))
                        ins.then_inc(S("s_prj"), 1)
                    for w in range(2):
                        r = uc * 4 + ot * 2 + w
                        if r >= 2:
                            t.wait_ge(S("s_rp1"), r - 1)
                        t.wait_ge(S("s_pev"), uc * 6 + ot * 3 + w + 1)
                        t.matmul(ps_c[:, r % 2, 0:256], swpb[:],
                                 qraw[:, r % 2, :], start=True,
                                 stop=True).then_inc(S("s_qsw"), 1)
                for tile in range(2):
                    gi = uc * 2 + tile
                    if gi >= 2:
                        t.wait_ge(S("s_gev"), gi - 1)
                    for kc in range(16):
                        t.matmul(ps_b[:, gi % 2, 0:256],
                                 us[:, uc % 2, kc, tile * 128:(tile + 1) * 128],
                                 wgb[:, kc, :], start=(kc == 0), stop=False,
                                 skip_group_check=True)
                        ins = t.matmul(ps_b[:, gi % 2, 256:260],
                                       us[:, uc % 2, kc,
                                          tile * 128:(tile + 1) * 128],
                                       wgkb[:, kc, :], start=False,
                                       stop=(kc == 15), skip_group_check=True)
                    ins.then_inc(S("s_gmm"), 1)
            t.wait_ge(S("s_lgd"), 1)
            for c in range(NCH):
                if c >= 2:
                    t.wait_ge(S("s_gse"), c - 1)
                t.matmul(ps_d[:, c % 2, 0:4], tri[:], gk_tm[:, c, :],
                         start=True, stop=False, skip_group_check=True)
                t.matmul(ps_d[:, c % 2, 4:8], suf[:], gk_tm[:, c, :],
                         start=False, stop=True,
                         skip_group_check=True).then_inc(S("s_gsm"), 1)
            t.wait_ge(S("s_bx"), 3)
            for c in range(NCH):
                for h in range(4):
                    u = c * 4 + h
                    ot, hh = h // 2, h % 2
                    ksl = k_fm[hh * 64:(hh + 1) * 64, ot, c * 128:(c + 1) * 128]
                    qsl = q_fm[hh * 64:(hh + 1) * 64, ot, c * 128:(c + 1) * 128]
                    vsl = v_fm[hh * 64:(hh + 1) * 64, ot, c * 128:(c + 1) * 128]
                    t.wait_ge(S("s_rsd"), u + 1)
                    if u >= 1:
                        t.wait_ge(S("s_dlg"), u)
                    t.matmul(ps_c[:, 0, 0:128], onesf[:], rsD[:, u % 2, :],
                             start=True, stop=True).then_inc(S("s_gib"), 1)
                    if u >= 1:
                        t.wait_ge(S("s_att"), u)
                    t.matmul(ps_c[:, 1, 0:128], ksl, qsl,
                             start=True, stop=True).then_inc(S("s_sct"), 1)
                    if u >= 2:
                        t.wait_ge(S("s_kv1"), 2 * (u - 1))
                        t.wait_ge(S("s_kv2"), u - 1)
                    t.transpose(pT[:, u % 2, 0:64], ksl,
                                id2b[hh * 64:(hh + 1) * 64, :])
                    t.transpose(pT[:, u % 2, 64:128], vsl,
                                id2b[hh * 64:(hh + 1) * 64, :]
                                ).then_inc(S("s_trp"), 1)
                    t.wait_ge(S("s_att"), u + 1)
                    t.wait_ge(S("s_kv1"), 2 * u + 1)
                    if h == 0 and c >= 2:
                        t.wait_ge(S("s_oev"), 8 * (c - 1))
                    if h == 0 and c >= 1:
                        t.wait_ge(S("s_pst"), 4 * c)
                        t.wait_ge(S("s_pbf"), 2 * c)
                    t.matmul(ps_a[:, c % 2, h * 64:(h + 1) * 64],
                             attb[:, u % 2, :], vtb[:, c % 2, h, :],
                             start=(h == 0), stop=False, skip_group_check=True)
                    ins = t.matmul(ps_b[:, c % 2, h * 64:(h + 1) * 64], qsl,
                                   P_bf[hh * 64:(hh + 1) * 64, ot, :],
                                   start=(h == 0), stop=(h == 3),
                                   skip_group_check=True)
                    ins.then_inc(S("s_oim"), 1)
                t.wait_ge(S("s_oev"), 8 * (c + 1))
                t.wait_ge(S("s_kv1"), 8 * (c + 1))
                t.wait_ge(S("s_kv2"), 4 * (c + 1))
                for ot in range(2):
                    for hh in range(2):
                        h = ot * 2 + hh
                        ins = t.matmul(ps_a[hh * 64:(hh + 1) * 64, c % 2,
                                            ot * 64:(ot + 1) * 64],
                                       ktb[:, c % 2, h, :], vdb[:, c % 2, h, :],
                                       start=(ot == 0), stop=(ot == 1),
                                       tile_position=(0, hh * 64),
                                       skip_group_check=True)
                    ins.then_inc(S("s_smm"), 1)

        @blk.vector
        def _(v):
            v.wait_ge(S("s_cn"), 8 * 16)
            v.tensor_copy(idb[:], cidf[:])
            v.tensor_copy(id2b[:], id2f[:])
            v.tensor_copy(wgkb[:], wgks[:])
            v.memset(epsc[:], 1e-6)
            v.memset(P_sb[:], 0.0)
            v.memset(P_bf[:], 0.0)
            v.nop().then_inc(S("s_gsv"), 1)
            for uc in range(NUC):
                for ot in range(2):
                    for w in range(3):
                        gidx = uc * 6 + ot * 3 + w
                        v.wait_ge(S("s_prj"), gidx + 1)
                        if w == 2:
                            v.tensor_copy(v_fm[:, ot, uc * 256:(uc + 1) * 256],
                                          ps_a[:, gidx % 2, 0:256]
                                          ).then_inc(S("s_pev"), 1)
                        else:
                            r = uc * 4 + ot * 2 + w
                            if r >= 2:
                                v.wait_ge(S("s_rt1"), r - 1)
                                v.wait_ge(S("s_qsw"), r - 1)
                            v.tensor_copy(qraw[:, r % 2, :],
                                          ps_a[:, gidx % 2, 0:256]
                                          ).then_inc(S("s_pev"), 1)
                            v.wait_ge(S("s_qsw"), r + 1)
                            v.wait_ge(S(f"s_cs{uc % 2}"), 32 * (uc // 2 + 1))
                            if r >= 2:
                                v.wait_ge(S("s_rp2"), r - 1)
                            v.tensor_tensor(t2b[:, r % 2, :],
                                            ps_c[:, r % 2, 0:256],
                                            snb[:, uc % 2, :],
                                            ALU.mult).then_inc(S("s_rp1"), 1)
                for tile in range(2):
                    gi = uc * 2 + tile
                    v.wait_ge(S("s_gmm"), gi + 1)
                    if gi >= 2:
                        v.wait_ge(S(f"s_gst{gi % 2}"), 16 * (gi // 2))
                    v.tensor_copy(gev[:, gi % 2, :], ps_b[:, gi % 2, 0:256])
                    v.tensor_copy(gkr[:, gi, :],
                                  ps_b[:, gi % 2, 256:260]).then_inc(S("s_gev"), 1)
            v.wait_ge(S("s_gev"), 2 * NUC)
            gv = gkr[:].opt()
            v.tensor_scalar(lg1[:], gv, 1.0 / 16, 0.0, ALU.mult,
                            ALU.min).then_inc(S("s_lgv"), 1)
            v.wait_ge(S("s_lgv"), 1)
            v.tensor_scalar(lg2[:], gv, -1.0, None, ALU.mult).then_inc(S("s_lgv"), 1)
            v.wait_ge(S("s_lgv"), 2)
            v.tensor_tensor(lg2[:], lg2[:], gv, ALU.max).then_inc(S("s_lgv"), 1)
            v.wait_ge(S("s_lgs"), 2)
            v.scalar_tensor_tensor(gk_tm[:].opt(), lg2[:], -1.0 / 16, lg1[:],
                                   ALU.mult, ALU.add).then_inc(S("s_lgd"), 1)
            for c in range(NCH):
                v.wait_ge(S("s_gsm"), c + 1)
                v.tensor_copy(GS[:, c, :], ps_d[:, c % 2, 0:8]).then_inc(S("s_gse"), 1)
            v.wait_ge(S("s_gse"), NCH)
            v.tensor_tensor(dsum[:].opt(), GS[:, :, 0:4], GS[:, :, 4:8],
                            ALU.add).then_inc(S("s_bat"), 1)
            v.nop().then_inc(S("s_bat"), 1)
            v.wait_ge(S("s_bx"), 3)
            for c in range(NCH):
                for h in range(4):
                    u = c * 4 + h
                    ot, hh = h // 2, h % 2
                    v.wait_ge(S("s_gib"), u + 1)
                    if u >= 2:
                        v.wait_ge(S("s_dex"), u - 1)
                    v.scalar_tensor_tensor(dlg[:, u % 2, :], ps_c[:, 0, 0:128],
                                           GS[:, c, h:h + 1], mska[:],
                                           ALU.subtract, ALU.add
                                           ).then_inc(S("s_dlg"), 1)
                    v.wait_ge(S("s_dex"), u + 1)
                    v.wait_ge(S("s_sct"), u + 1)
                    if u >= 1:
                        v.wait_ge(S("s_oim"), 4 * c + h)
                    v.tensor_tensor(attb[:, u % 2, :], ps_c[:, 1, 0:128],
                                    dexb[:, u % 2, :],
                                    ALU.mult).then_inc(S("s_att"), 1)
                    v.wait_ge(S("s_trp"), u + 1)
                    if c >= 2:
                        v.wait_ge(S("s_smm"), 2 * (c - 1))
                    v.tensor_copy(vtb[:, c % 2, h, :],
                                  pT[:, u % 2, 64:128]).then_inc(S("s_kv1"), 1)
                    v.tensor_scalar(vdb[:, c % 2, h, :], pT[:, u % 2, 64:128],
                                    eS_all[:, c, h:h + 1], None,
                                    ALU.mult).then_inc(S("s_kv1"), 1)
                v.wait_ge(S("s_oim"), 4 * (c + 1))
                for h in range(4):
                    u = c * 4 + h
                    if c >= 2 and h == 0:
                        v.wait_ge(S("s_oo"), c - 1)
                    v.tensor_copy(o_sb[:, c % 2, h, :],
                                  ps_a[:, c % 2, h * 64:(h + 1) * 64]
                                  ).then_inc(S("s_oev"), 1)
                    v.wait_ge(S("s_oev"), 2 * u + 1)
                    v.scalar_tensor_tensor(o_sb[:, c % 2, h, :],
                                           ps_b[:, c % 2, h * 64:(h + 1) * 64],
                                           eG_all[:, c, h:h + 1],
                                           o_sb[:, c % 2, h, :], ALU.mult,
                                           ALU.add).then_inc(S("s_oev"), 1)
                for ot in range(2):
                    v.wait_ge(S("s_smm"), 2 * (c + 1))
                    for hh in range(2):
                        h = ot * 2 + hh
                        v.scalar_tensor_tensor(
                            P_sb[hh * 64:(hh + 1) * 64, ot, :],
                            P_sb[hh * 64:(hh + 1) * 64, ot, :],
                            dex_all[hh * 64:(hh + 1) * 64, c, h:h + 1],
                            ps_a[hh * 64:(hh + 1) * 64, c % 2,
                                 ot * 64:(ot + 1) * 64],
                            ALU.mult, ALU.add).then_inc(S("s_pst"), 1)
                    v.wait_ge(S("s_pst"), 4 * c + 2 * (ot + 1))
                    v.tensor_copy(P_bf[:, ot, :],
                                  P_sb[:, ot, :]).then_inc(S("s_pbf"), 1)
                v.tensor_tensor(osc[:], o_sb[:, c % 2, :, :].opt(),
                                o_sb[:, c % 2, :, :].opt(), ALU.mult
                                ).then_inc(S("s_gt1"), 1)
                v.wait_ge(S("s_gt1"), 2 * c + 1)
                v.reduce_sum(nrm[:, c % 2, 0:4],
                             osc[:].rearrange("p (h d) -> p h d", d=64),
                             axis=AX.X).then_inc(S("s_gt1"), 1)
                v.wait_ge(S("s_rst"), 2 * (c + 1))
                v.tensor_tensor(ob1[:].rearrange("p (h d) -> p h d", d=64),
                                o_sb[:, c % 2, :, :],
                                nrm[:, c % 2, 8:12].unsqueeze(2)
                                .broadcast_to([128, 4, 64]),
                                ALU.mult).then_inc(S("s_ob1"), 1)
                v.wait_ge(S("s_ob1"), c + 1)
                v.wait_ge(S("s_sgl"), c + 1)
                if c >= 2:
                    v.wait_ge(S(f"s_ost{c % 2}"), 16 * (c // 2))
                v.tensor_tensor(oob[:, c % 2, :], ob1[:], sgb[:, c % 2, :],
                                ALU.mult).then_inc(S("s_oo"), 1)

        @blk.gpsimd
        def _(g_):
            for uc in range(NUC):
                for ot in range(2):
                    for w in range(2):
                        r = uc * 4 + ot * 2 + w
                        g_.wait_ge(S("s_pev"), uc * 6 + ot * 3 + w + 1)
                        g_.wait_ge(S(f"s_cs{uc % 2}"), 32 * (uc // 2 + 1))
                        if r >= 2:
                            g_.wait_ge(S("s_rp2"), r - 1)
                        g_.tensor_tensor(t1b[:, r % 2, :], qraw[:, r % 2, :],
                                         csb[:, uc % 2, :],
                                         ALU.mult).then_inc(S("s_rt1"), 1)
                        g_.wait_ge(S("s_rt1"), r + 1)
                        g_.wait_ge(S("s_rp1"), r + 1)
                        dst = q_fm if w == 0 else k_fm
                        g_.tensor_tensor(dst[:, ot, uc * 256:(uc + 1) * 256],
                                         t1b[:, r % 2, :], t2b[:, r % 2, :],
                                         ALU.add).then_inc(S("s_rp2"), 1)
                for tile in range(2):
                    gi = uc * 2 + tile
                    g_.wait_ge(S("s_gev"), gi + 1)
                    g_.dma_start(out=g_dram[gi],
                                 in_=gev[:, gi % 2, :]).then_inc(
                                     S(f"s_gst{gi % 2}"), 16)
            g_.wait_ge(S("s_lgd"), 1)
            for c in range(NCH):
                for h in range(4):
                    u = c * 4 + h
                    if u >= 2:
                        g_.wait_ge(S("s_gib"), u - 1)
                    g_.tensor_scalar(rsD[:, u % 2, :], tri[:],
                                     gk_tm[:, c, h:h + 1], None,
                                     ALU.mult).then_inc(S("s_rsd"), 1)
                g_.wait_ge(S("s_oo"), c + 1)
                g_.dma_start(out=o_out[c],
                             in_=oob[:, c % 2, :]).then_inc(S(f"s_ost{c % 2}"), 16)


def build_b2():
    NUC, NCH = 32, 64
    nc = bass.Bass("TRN2", target_bir_lowering=False)
    ext = {}
    ext["u_in"] = nc.dram_tensor("u_in", [NUC, 128, 16, 256], BF16, kind="ExternalInput").ap()
    ext["wq_in"] = nc.dram_tensor("wq_in", [2, 16, 128, 128], F32, kind="ExternalInput").ap()
    ext["wk_in"] = nc.dram_tensor("wk_in", [2, 16, 128, 128], F32, kind="ExternalInput").ap()
    ext["wv_in"] = nc.dram_tensor("wv_in", [2, 16, 128, 128], F32, kind="ExternalInput").ap()
    ext["wg_in"] = nc.dram_tensor("wg_in", [16, 128, 256], F32, kind="ExternalInput").ap()
    ext["wgk_in"] = nc.dram_tensor("wgk_in", [128, 16, 4], F32, kind="ExternalInput").ap()
    ext["cos_in"] = nc.dram_tensor("cos_in", [NUC, 128, 256], BF16, kind="ExternalInput").ap()
    ext["sin_in"] = nc.dram_tensor("sin_in", [NUC, 128, 256], BF16, kind="ExternalInput").ap()
    ext["swp_in"] = nc.dram_tensor("swp_in", [128, 128], BF16, kind="ExternalInput").ap()
    ext["tri_in"] = nc.dram_tensor("tri_in", [128, 128], F32, kind="ExternalInput").ap()
    ext["suf_in"] = nc.dram_tensor("suf_in", [128, 128], F32, kind="ExternalInput").ap()
    ext["one_in"] = nc.dram_tensor("one_in", [128, 128], F32, kind="ExternalInput").ap()
    ext["msk_in"] = nc.dram_tensor("msk_in", [128, 128], F32, kind="ExternalInput").ap()
    ext["id_in"] = nc.dram_tensor("id_in", [128, 128], F32, kind="ExternalInput").ap()
    ext["id2_in"] = nc.dram_tensor("id2_in", [128, 64], F32, kind="ExternalInput").ap()
    ext["o_out"] = nc.dram_tensor("o_out", [NCH, 128, 256], BF16, kind="ExternalOutput").ap()
    ext["g_dram"] = nc.dram_tensor("g_dram", [NCH, 128, 256], BF16).ap()
    emit_b2(nc, ext)
    return nc



def _b2_inmaps(Wq, Wk, Wv, Wg, Wgk):
    ident = np.eye(128, dtype=np.float32)
    tri = np.tril(np.ones((128, 128), np.float32), 0).T
    suf = 1.0 - tri
    onesf = np.ones((128, 128), np.float32)
    mska = np.where(np.arange(128)[None, :] >= np.arange(128)[:, None],
                    np.float32(np.log(1.0 / 8.0)), np.float32(-1e30)
                    ).astype(np.float32)
    swp = np.zeros((128, 128), np.float32)
    for blk0 in range(0, 128, 64):
        for i in range(32):
            swp[blk0 + 32 + i, blk0 + i] = -1.0
            swp[blk0 + i, blk0 + 32 + i] = 1.0
    swp = swp.astype(BF16NP)
    inv = (10000.0 ** (-np.arange(0, 64, 2, dtype=np.float64) / 64))
    tpos = np.arange(T, dtype=np.float64)
    ang = tpos[:, None] * inv[None, :]
    cosT = np.cos(ang).astype(np.float32)
    sinT = np.sin(ang).astype(np.float32)
    cosS = np.ascontiguousarray(
        np.tile(cosT.reshape(32, 256, 32).transpose(0, 2, 1), (1, 4, 1))
    ).astype(BF16NP)
    sinS = np.ascontiguousarray(
        np.tile(sinT.reshape(32, 256, 32).transpose(0, 2, 1), (1, 4, 1))
    ).astype(BF16NP)

    def lhsT_tiles(Wrows):
        wt = _f32(Wrows).T.reshape(16, 128, 2, 128).transpose(2, 0, 1, 3)
        return np.ascontiguousarray(wt)

    in_maps = []
    for c in range(NC):
        rows = []
        for hp in range(2):
            base = (4 * c + 2 * hp) * 64
            for hh in range(2):
                b2 = base + hh * 64
                rows += list(range(b2, b2 + 64, 2))
                rows += list(range(b2 + 1, b2 + 64, 2))
        rows = np.asarray(rows)
        natural = np.arange(256) + 256 * c
        wq_t = lhsT_tiles(Wq[rows])
        wk_t = lhsT_tiles(Wk[rows])
        wv_t = lhsT_tiles(Wv[natural])
        wg_t = np.ascontiguousarray(_f32(Wg[natural]).T.reshape(16, 128, 256))
        wgk_t = np.ascontiguousarray(
            _f32(Wgk[4 * c:4 * c + 4]).T.reshape(16, 128, 4).transpose(1, 0, 2))
        in_maps.append({
            "wq_in": wq_t, "wk_in": wk_t, "wv_in": wv_t,
            "wg_in": wg_t, "wgk_in": wgk_t, "cos_in": cosS, "sin_in": sinS,
            "swp_in": swp, "tri_in": np.ascontiguousarray(tri),
            "suf_in": np.ascontiguousarray(suf), "one_in": onesf,
            "msk_in": mska,
            "id2_in": np.ascontiguousarray(
                np.vstack([np.eye(64), np.eye(64)]).astype(np.float32)),
        })
    return in_maps


def _c2_inmaps(in_proj_w, conv_w, conv_b, dt_bias, A_log, Dp):
    ident = np.eye(128, dtype=np.float32)
    tri = np.ascontiguousarray(np.tril(np.ones((128, 128), np.float32), 0).T)
    suf = np.ascontiguousarray(1.0 - tri)
    onesf = np.ones((128, 128), np.float32)
    mska = np.where(np.arange(128)[None, :] >= np.arange(128)[:, None],
                    np.float32(0.0), np.float32(-1e30)).astype(np.float32)
    in_maps = []
    for c in range(NC):
        xs_rows = np.arange(4096 + 512 * c, 4096 + 512 * (c + 1))
        bc_rows = np.arange(8192, 8320)
        rows = np.concatenate([xs_rows, bc_rows])
        W = _f32(in_proj_w[rows])
        wxs = np.ascontiguousarray(
            W.T.reshape(16, 128, 5, 128).transpose(2, 1, 0, 3))
        wg = np.ascontiguousarray(
            _f32(in_proj_w[512 * c:512 * (c + 1)]).T.reshape(16, 128, 512))
        wdt = np.ascontiguousarray(
            _f32(in_proj_w[8320 + 8 * c:8320 + 8 * (c + 1)]).T.reshape(16, 128, 8))
        ch = np.concatenate([np.arange(512 * c, 512 * (c + 1)),
                             np.arange(4096, 4224)])
        wcv = np.ascontiguousarray(
            _f32(conv_w[ch]).reshape(5, 128, 4).transpose(1, 0, 2))
        bcv = np.ascontiguousarray(
            _f32(conv_b[ch]).reshape(5, 128).T)
        in_maps.append({
            "wxs_in": wxs, "wg_in": wg, "wdt_in": wdt,
            "wcv_in": wcv, "bcv_in": bcv,
            "dtb_in": _f32(dt_bias[8 * c:8 * (c + 1)]).reshape(1, 8),
            "alog_in": _f32(A_log[8 * c:8 * (c + 1)]).reshape(1, 8),
            "dd_in": _f32(Dp[8 * c:8 * (c + 1)]).reshape(1, 8),
            "tri_in": tri, "suf_in": suf, "one_in": onesf, "msk_in": mska,
        })
    return in_maps


def run_b2(u_full, Wq, Wk, Wv, Wg, Wgk):
    nc = _prog("b2", build_b2)
    u_r = np.ascontiguousarray(
        u_full.reshape(16, 128, 32, 256).transpose(2, 1, 0, 3))
    ident = np.eye(128, dtype=np.float32)
    tri = np.tril(np.ones((128, 128), np.float32), 0).T
    suf = 1.0 - tri
    onesf = np.ones((128, 128), np.float32)
    mska = np.where(np.arange(128)[None, :] >= np.arange(128)[:, None],
                    np.float32(np.log(1.0 / 8.0)), np.float32(-1e30)
                    ).astype(np.float32)
    swp = np.zeros((128, 128), np.float32)
    for blk0 in range(0, 128, 64):
        for i in range(32):
            swp[blk0 + 32 + i, blk0 + i] = -1.0
            swp[blk0 + i, blk0 + 32 + i] = 1.0
    swp = swp.astype(BF16NP)
    inv = (10000.0 ** (-np.arange(0, 64, 2, dtype=np.float64) / 64))
    tpos = np.arange(T, dtype=np.float64)
    ang = tpos[:, None] * inv[None, :]
    cosT = np.cos(ang).astype(np.float32)
    sinT = np.sin(ang).astype(np.float32)
    cosS = np.ascontiguousarray(
        np.tile(cosT.reshape(32, 256, 32).transpose(0, 2, 1), (1, 4, 1))
    ).astype(BF16NP)
    sinS = np.ascontiguousarray(
        np.tile(sinT.reshape(32, 256, 32).transpose(0, 2, 1), (1, 4, 1))
    ).astype(BF16NP)

    def lhsT_tiles(Wrows):
        wt = _f32(Wrows).T.reshape(16, 128, 2, 128).transpose(2, 0, 1, 3)
        return np.ascontiguousarray(wt)

    in_maps = []
    for c in range(NC):
        rows = []
        for hp in range(2):
            base = (4 * c + 2 * hp) * 64
            for hh in range(2):
                b2 = base + hh * 64
                rows += list(range(b2, b2 + 64, 2))
                rows += list(range(b2 + 1, b2 + 64, 2))
        rows = np.asarray(rows)
        natural = np.arange(256) + 256 * c
        wq_t = lhsT_tiles(Wq[rows])
        wk_t = lhsT_tiles(Wk[rows])
        wv_t = lhsT_tiles(Wv[natural])
        wg_t = np.ascontiguousarray(_f32(Wg[natural]).T.reshape(16, 128, 256))
        wgk_t = np.ascontiguousarray(
            _f32(Wgk[4 * c:4 * c + 4]).T.reshape(16, 128, 4).transpose(1, 0, 2))
        in_maps.append({
            "u_in": u_r, "wq_in": wq_t, "wk_in": wk_t, "wv_in": wv_t,
            "wg_in": wg_t, "wgk_in": wgk_t, "cos_in": cosS, "sin_in": sinS,
            "swp_in": swp, "tri_in": np.ascontiguousarray(tri),
            "suf_in": np.ascontiguousarray(suf), "one_in": onesf,
            "msk_in": mska, "id_in": ident,
            "id2_in": np.ascontiguousarray(
                np.vstack([np.eye(64), np.eye(64)]).astype(np.float32)),
        })
    res = run_bass_kernel_spmd(nc, in_maps, core_ids=CORE_IDS)
    return np.concatenate(
        [r["o_out"].reshape(T, 256) for r in res.results], axis=1)



def emit_c2(nc, ext, tag=""):
    NCH = 64
    NUC = 32
    sb = lambda n, s, d: nc.sbuf_tensor(tag + n, s, d)
    ps = lambda n, s, d: nc.psum_tensor(tag + n, s, d)
    u_in = ext["u_in"]
    wxs_in = ext["wxs_in"]
    wg_in = ext["wg_in"]
    wdt_in = ext["wdt_in"]
    wcv_in = ext["wcv_in"]
    bcv_in = ext["bcv_in"]
    dtb_in = ext["dtb_in"]
    alog_in = ext["alog_in"]
    dd_in = ext["dd_in"]
    tri_in = ext["tri_in"]
    suf_in = ext["suf_in"]
    one_in = ext["one_in"]
    msk_in = ext["msk_in"]
    id_in = ext["id_in"]
    y_out = ext["y_out"]
    g_dram = ext["g_dram"]
    xs_dram = ext["xs_dram"]

    with ExitStack() as ctx:
        ent = ctx.enter_context
        us = ent(sb("us", [128, 2, 16, 256], BF16))
        raw = ent(sb("raw", [128, 2, 5, 259], BF16))
        cacc = ent(sb("cacc", [128, 1, 5, 256], F32))
        cvo = ent(sb("cvo", [128, 2, 5, 256], BF16))
        xs_ev = ent(sb("xs_ev", [128, 2, 512], BF16))
        xrl = ent(sb("xrl", [128, 2, 512], BF16))
        B_fm = ent(sb("B_fm", [64, T], BF16))
        C0_fm = ent(sb("C0_fm", [64, T], BF16))
        dtraw = ent(sb("dtraw", [128, NCH, 8], F32))
        dt_tm = ent(sb("dt_tm", [128, NCH, 8], F32))
        dA_tm = ent(sb("dA_tm", [128, NCH, 8], F32))

        AS_tm = ent(sb("AS_tm", [128, NCH, 16], F32))
        dec_all = ent(sb("dec_all", [128, NCH, 8], F32))
        eG_all = ent(sb("eG_all", [128, NCH, 8], F32))
        eS_all = ent(sb("eS_all", [128, NCH, 8], F32))
        lgs1 = ent(sb("lgs1", [128, NCH, 8], F32))
        lgs2 = dA_tm
        wxb = ent(sb("wxb", [128, 5, 16, 128], BF16))
        wgb = ent(sb("wgb", [128, 16, 512], BF16))
        wdtb = ent(sb("wdtb", [128, 16, 8], BF16))
        wstg = ent(sb("wstg", [128, 1, 2048], F32))
        wcv = ent(sb("wcv", [128, 5, 4], F32))
        bcv = ent(sb("bcv", [128, 5], F32))
        tinyr = ent(sb("tinyr", [1, 3, 8], F32))
        dtb_bc = ent(sb("dtb_bc", [128, 8], F32))
        A_bc = ent(sb("A_bc", [128, 8], F32))
        D_bc = ent(sb("D_bc", [128, 8], F32))
        tri = ent(sb("tri", [128, 128], F32))
        suf = ent(sb("suf", [128, 128], F32))
        onesf = ent(sb("onesf", [128, 128], F32))
        mska = ent(sb("mska", [128, 128], F32))
        cidf = ent(sb("cidf", [128, 128], F32))
        idb = ent(sb("idb", [128, 128], BF16))
        rsD = ent(sb("rsD", [128, 2, 128], F32))
        dlg = ent(sb("dlg", [128, 2, 128], F32))
        dexb = ent(sb("dexb", [128, 2, 128], BF16))
        attb = ent(sb("attb", [128, 2, 128], BF16))
        btb = ent(sb("btb", [128, 2, 64], BF16))
        xdt = ent(sb("xdt", [128, 2, 512], BF16))
        xdd = ent(sb("xdd", [128, 2, 512], BF16))
        P_sb = ent(sb("P_sb", [64, 8, 64], F32))
        P_bf = ent(sb("P_bf", [64, 8, 64], BF16))
        ysb = ent(sb("ysb", [128, 512], F32))
        yt2 = ent(sb("yt2", [128, 512], F32))
        yt3 = ent(sb("yt3", [128, 512], F32))
        gev = ent(sb("gev", [128, 2, 512], BF16))
        grl = ent(sb("grl", [128, 2, 512], BF16))
        sgb = ent(sb("sgb", [128, 2, 512], BF16))
        yob = ent(sb("yob", [128, 2, 512], BF16))
        ctmp = ent(sb("ctmp", [128, 256], F32))

        psA = ent(ps("psA", [128, 2, 512], F32))
        psB = ent(ps("psB", [128, 512], F32))
        psC = ent(ps("psC", [128, 2, 512], F32))
        psD = ent(ps("psD", [128, 512], F32))
        psE = ent(ps("psE", [64, 512], F32))
        pTb = psB.ap().bitcast(BF16)
        pTd = psD.ap().bitcast(BF16)

        sem = {}
        for name in ["s_cn", "s_wst0", "s_wst1", "s_wrdy", "s_ul0", "s_ul1",
                     "s_prj", "s_pev", "s_cvh", "s_cvt", "s_cva", "s_cvs",
                     "s_trx", "s_xse", "s_bce", "s_c00", "s_c01", "s_gmm", "s_gev",
                     "s_gst0", "s_gst1", "s_xst0", "s_xst1", "s_xrl0",
                     "s_xrl1", "s_cvb", "s_dte", "s_asm", "s_ase", "s_bx",
                     "s_tin", "s_rsd", "s_gib", "s_dlg", "s_dex", "s_cbt",
                     "s_att", "s_trp", "s_bte", "s_xde", "s_yim", "s_yev",
                     "s_smm", "s_pst", "s_pbf", "s_sgl", "s_grl0", "s_grl1",
                     "s_yo", "s_ost0", "s_ost1", "s_gsv", "s_vy", "s_gy"]:
            sem[name] = ent(nc.semaphore(name))
        S = lambda n: sem[n]
        blk = ent(nc.Block())

        @blk.sync
        def _(q):
            for dst, ap_ in ((cidf, id_in), (tri, tri_in), (suf, suf_in),
                             (onesf, one_in), (mska, msk_in), (wcv, wcv_in),
                             (bcv, bcv_in)):
                q.dma_start(out=dst[:], in_=ap_[:]).then_inc(S("s_cn"), 16)
            q.dma_start(out=tinyr[:, 0, :], in_=dtb_in[:]).then_inc(S("s_cn"), 16)
            q.dma_start(out=tinyr[:, 1, :], in_=alog_in[:]).then_inc(S("s_cn"), 16)
            q.dma_start(out=tinyr[:, 2, :], in_=dd_in[:]).then_inc(S("s_cn"), 16)
            for b in range(10):
                if b >= 1:
                    q.wait_ge(S("s_wrdy"), b)
                if b < 5:
                    q.dma_start(out=wstg[:, 0, :].rearrange(
                                    "p (kc c) -> p kc c", c=128),
                                in_=wxs_in[b]).then_inc(S("s_wst0"), 16)
                elif b < 9:
                    qt = b - 5
                    q.dma_start(out=wstg[:, 0, :].rearrange(
                                    "p (kc c) -> p kc c", c=512),
                                in_=wg_in[qt * 4:(qt + 1) * 4]
                                .rearrange("kc p c -> p kc c")
                                ).then_inc(S("s_wst0"), 16)
                else:
                    q.dma_start(out=wstg[:, 0, 0:128].rearrange(
                                    "p (kc c) -> p kc c", c=8),
                                in_=wdt_in[:].rearrange("kc p c -> p kc c")
                                ).then_inc(S("s_wst0"), 16)
            for uc in range(NUC):
                if uc >= 2:
                    q.wait_ge(S("s_gmm"), 2 * (uc - 1))
                q.dma_start(out=us[:, uc % 2, :, :],
                            in_=u_in[uc]).then_inc(S(f"s_ul{uc % 2}"), 16)
            for c in range(NCH):
                if c >= 2:
                    q.wait_ge(S("s_sgl"), c - 1)
                    q.wait_ge(S("s_xde"), 2 * (c - 1))
                    q.wait_ge(S("s_yev"), 3 * (c - 1))
                q.wait_ge(S(f"s_gst{c % 2}"), 16 * (c // 2 + 1))
                q.dma_start(out=grl[:, c % 2, :],
                            in_=g_dram[c]).then_inc(S(f"s_grl{c % 2}"), 16)
                q.wait_ge(S(f"s_xst{c % 2}"), 16 * (c // 2 + 1))
                q.dma_start(out=xrl[:, c % 2, :],
                            in_=xs_dram[c]).then_inc(S(f"s_xrl{c % 2}"), 16)
            for i in range(2):
                q.wait_ge(S(f"s_ost{i}"), 16 * (NCH // 2))

        @blk.scalar
        def _(s):
            for b in range(10):
                s.wait_ge(S("s_wst0"), 16 * (b + 1))
                if b < 5:
                    s.activation(wxb[:, b, :, :], wstg[:, 0, :],
                                 AF.Copy).then_inc(S("s_wrdy"), 1)
                elif b < 9:
                    qt = b - 5
                    s.activation(wgb[:, qt * 4:(qt + 1) * 4, :],
                                 wstg[:, 0, :],
                                 AF.Copy).then_inc(S("s_wrdy"), 1)
                else:
                    s.activation(wdtb[:], wstg[:, 0, 0:128],
                                 AF.Copy).then_inc(S("s_wrdy"), 1)
            s.wait_ge(S("s_cn"), 10 * 16)
            s.activation(tinyr[:, 1, :], tinyr[:, 1, :],
                         AF.Exp).then_inc(S("s_tin"), 1)
            for uc in range(NUC):
                for ct in range(5):
                    if ct != 2:
                        k = ct if ct < 2 else ct - 1
                        s.wait_ge(S("s_cva"), 16 * uc + 4 * (k + 1))
                    else:
                        s.wait_ge(S("s_cvb"), 7 * uc + 7)
                    if uc >= 2:
                        s.wait_ge(S("s_trx"), 8 * (uc - 1))
                        s.wait_ge(S("s_bce"), 2 * (uc - 1))
                        s.wait_ge(S("s_c00"), 16 * (uc - 1))
                        s.wait_ge(S("s_c01"), 16 * (uc - 1))
                    s.activation(cvo[:, uc % 2, ct, :], cacc[:, 0, ct, :],
                                 AF.Silu, bias=bcv[:, ct:ct + 1]
                                 ).then_inc(S("s_cvs"), 1)
                for tile in range(2):
                    c = uc * 2 + tile
                    s.wait_ge(S("s_cvs"), 5 * (uc + 1))
                    s.wait_ge(S(f"s_c0{c % 2}"), 16 * (c // 2))
                    s.dma_start(out=C0_fm[:, c * 128:(c + 1) * 128],
                                in_=cvo[64:128, uc % 2, 4,
                                        tile * 128:(tile + 1) * 128]
                                ).then_inc(S(f"s_c0{c % 2}"), 16)
            s.wait_ge(S("s_dte"), 3)
            s.activation(lgs2[:].opt(), lgs2[:].opt(), AF.Exp,
                         scale=-1.0).then_inc(S("s_dte"), 1)
            s.wait_ge(S("s_dte"), 4)
            s.activation(lgs2[:].opt(), lgs2[:].opt(), AF.Ln,
                         bias=1.0).then_inc(S("s_dte"), 1)
            s.wait_ge(S("s_bx"), 1)
            s.activation(dec_all[:].opt(), lgs1[:].opt(),
                         AF.Exp).then_inc(S("s_bx"), 1)
            s.activation(eG_all[:].opt(), AS_tm[:, :, 0:8],
                         AF.Exp).then_inc(S("s_bx"), 1)
            s.activation(eS_all[:].opt(), AS_tm[:, :, 8:16],
                         AF.Exp).then_inc(S("s_bx"), 1)
            for c in range(NCH):
                for h in range(8):
                    u = c * 8 + h
                    s.wait_ge(S("s_dlg"), u + 1)
                    if u >= 2:
                        s.wait_ge(S("s_att"), u - 1)
                    s.activation(dexb[:, u % 2, :], dlg[:, u % 2, :],
                                 AF.Exp).then_inc(S("s_dex"), 1)
                s.wait_ge(S("s_yim"), 8 * (c + 1))
                if c >= 2:
                    s.wait_ge(S("s_yo"), c - 1)
                s.activation(ysb[:], psA[:, c % 2, :],
                             AF.Copy).then_inc(S("s_yev"), 1)
                s.wait_ge(S(f"s_grl{c % 2}"), 16 * (c // 2 + 1))
                s.activation(sgb[:, c % 2, :], grl[:, c % 2, :],
                             AF.Silu).then_inc(S("s_sgl"), 1)

        @blk.tensor
        def _(t):
            t.wait_ge(S("s_cn"), 10 * 16)
            t.wait_ge(S("s_wrdy"), 10)
            t.wait_ge(S("s_gsv"), 1)
            t.wait_ge(S("s_tin"), 1)
            t.matmul(psD[:, 0:8], onesf[0:1, :], tinyr[:, 0, :],
                     start=True, stop=False, skip_group_check=True)
            t.matmul(psD[:, 8:16], onesf[0:1, :], tinyr[:, 1, :],
                     start=False, stop=False, skip_group_check=True)
            t.matmul(psD[:, 16:24], onesf[0:1, :], tinyr[:, 2, :],
                     start=False, stop=True,
                     skip_group_check=True).then_inc(S("s_tin"), 1)
            for uc in range(NUC):
                t.wait_ge(S(f"s_ul{uc % 2}"), 16 * (uc // 2 + 1))
                for ot in range(5):
                    gidx = uc * 5 + ot
                    if gidx >= 2:
                        t.wait_ge(S("s_pev"), gidx - 1)
                    for kc in range(16):
                        ins = t.matmul(psA[:, gidx % 2, 0:256],
                                       wxb[:, ot, kc, :],
                                       us[:, uc % 2, kc, :],
                                       start=(kc == 0), stop=(kc == 15))
                    ins.then_inc(S("s_prj"), 1)
                for tile in range(2):
                    gi = uc * 2 + tile
                    if gi >= 1:
                        t.wait_ge(S("s_gev"), gi)
                    for kc in range(16):
                        t.matmul(psC[:, 0, :],
                                 us[:, uc % 2, kc, tile * 128:(tile + 1) * 128],
                                 wgb[:, kc, :], start=(kc == 0), stop=(kc == 15),
                                 skip_group_check=True)
                        ins = t.matmul(psC[:, 1, 0:8],
                                       us[:, uc % 2, kc, tile * 128:(tile + 1) * 128],
                                       wdtb[:, kc, :], start=(kc == 0),
                                       stop=(kc == 15), skip_group_check=True)
                    ins.then_inc(S("s_gmm"), 1)
                t.wait_ge(S("s_cvs"), 5 * (uc + 1) - 1)
                for tile in range(2):
                    c = uc * 2 + tile
                    for ct in range(4):
                        tx = c * 4 + ct
                        if tx >= 2:
                            t.wait_ge(S("s_xse"), tx - 1)
                        dst = pTb[:, 0:128] if tx % 2 == 0 else pTd[:, 0:128]
                        t.transpose(dst,
                                    cvo[:, uc % 2, ct, tile * 128:(tile + 1) * 128],
                                    idb[:]).then_inc(S("s_trx"), 1)
            t.wait_ge(S("s_dte"), 8)
            for c in range(NCH):
                if c >= 2:
                    t.wait_ge(S("s_ase"), c - 1)
                t.matmul(psA[:, c % 2, 0:8], tri[:], dA_tm[:, c, :],
                         start=True, stop=False, skip_group_check=True)
                t.matmul(psA[:, c % 2, 8:16], suf[:], dA_tm[:, c, :],
                         start=False, stop=True,
                         skip_group_check=True).then_inc(S("s_asm"), 1)
            t.wait_ge(S("s_bx"), 4)
            for c in range(NCH):
                t.wait_ge(S("s_att"), 8 * c)
                t.wait_ge(S("s_bce"), c + 1)
                t.wait_ge(S(f"s_c0{c % 2}"), 16 * (c // 2 + 1))
                t.matmul(psC[:, 0, 0:128], B_fm[:, c * 128:(c + 1) * 128],
                         C0_fm[:, c * 128:(c + 1) * 128],
                         start=True, stop=True).then_inc(S("s_cbt"), 1)
                if c >= 1:
                    t.wait_ge(S("s_bte"), c)
                t.transpose(pTd[:, 256:320], B_fm[:, c * 128:(c + 1) * 128],
                            idb[0:64, 0:64]).then_inc(S("s_trp"), 1)
                for h in range(8):
                    u = c * 8 + h
                    t.wait_ge(S("s_rsd"), u + 1)
                    if u >= 1:
                        t.wait_ge(S("s_dlg"), u)
                    t.matmul(psC[:, 1, 0:128], onesf[:], rsD[:, u % 2, :],
                             start=True, stop=True).then_inc(S("s_gib"), 1)
                    t.wait_ge(S("s_att"), u + 1)
                    if h == 0:
                        t.wait_ge(S("s_xde"), 2 * c + 1)
                        if c >= 2:
                            t.wait_ge(S("s_yev"), 3 * (c - 2) + 1)
                        if c >= 1:
                            t.wait_ge(S("s_yev"), 3 * (c - 1) + 2)
                            t.wait_ge(S("s_pbf"), c)
                    t.matmul(psA[:, c % 2, h * 64:(h + 1) * 64],
                             attb[:, u % 2, :],
                             xdt[:, c % 2, h * 64:(h + 1) * 64],
                             start=(h == 0), stop=False, skip_group_check=True)
                    ins = t.matmul(psB[:, h * 64:(h + 1) * 64],
                                   C0_fm[:, c * 128:(c + 1) * 128],
                                   P_bf[:, h, :],
                                   start=(h == 0), stop=(h == 7),
                                   skip_group_check=True)
                    ins.then_inc(S("s_yim"), 1)
                t.wait_ge(S("s_pst"), 2 * c)
                t.wait_ge(S("s_xde"), 2 * (c + 1))
                t.wait_ge(S("s_bte"), c + 1)
                for h in range(8):
                    ins = t.matmul(psE[:, h * 64:(h + 1) * 64],
                                   btb[:, c % 2, :],
                                   xdd[:, c % 2, h * 64:(h + 1) * 64],
                                   start=(h == 0), stop=(h == 7),
                                   skip_group_check=True)
                ins.then_inc(S("s_smm"), 1)

        @blk.vector
        def _(v):
            v.wait_ge(S("s_cn"), 10 * 16)
            v.tensor_copy(idb[:], cidf[:])
            v.memset(P_sb[:], 0.0)
            v.memset(P_bf[:], 0.0)
            v.memset(raw[:, 1, :, 256:259], 0.0).then_inc(S("s_gsv"), 1)
            v.wait_ge(S("s_tin"), 2)
            v.tensor_copy(dtb_bc[:], psD[:, 0:8])
            v.tensor_scalar(A_bc[:], psD[:, 8:16], -1.0, None, ALU.mult)
            v.tensor_copy(D_bc[:], psD[:, 16:24]).then_inc(S("s_tin"), 1)
            for uc in range(NUC):
                v.wait_ge(S("s_prj"), uc * 5 + 1)
                if uc == 0:
                    v.wait_ge(S("s_gsv"), 1)
                if uc >= 2:
                    v.wait_ge(S("s_cva"), 16 * (uc - 1))
                    v.wait_ge(S("s_cvb"), 7 * (uc - 1))
                v.tensor_copy(raw[:, uc % 2, :, 0:3],
                              raw[:, (uc + 1) % 2, :, 256:259]
                              ).then_inc(S("s_cvh"), 1)
                for ot in range(5):
                    gidx = uc * 5 + ot
                    v.wait_ge(S("s_prj"), gidx + 1)
                    v.tensor_copy(raw[:, uc % 2, ot, 3:259],
                                  psA[:, gidx % 2, 0:256]).then_inc(S("s_pev"), 1)
                v.nop().then_inc(S("s_cvt"), 1)
                v.wait_ge(S("s_pev"), 5 * (uc + 1))
                v.wait_ge(S("s_cvh"), uc + 1)
                for k, ct in enumerate((0, 1, 3, 4)):
                    base = 16 * uc + 4 * k
                    if uc >= 1:
                        v.wait_ge(S("s_cvs"), 5 * uc)
                    v.tensor_scalar(cacc[:, 0, ct, :],
                                    raw[:, uc % 2, ct, 0:256],
                                    wcv[:, ct, 0:1], None,
                                    ALU.mult).then_inc(S("s_cva"), 1)
                    for tau in range(1, 4):
                        v.wait_ge(S("s_cva"), base + tau)
                        v.scalar_tensor_tensor(cacc[:, 0, ct, :],
                                               raw[:, uc % 2, ct, tau:tau + 256],
                                               wcv[:, ct, tau:tau + 1],
                                               cacc[:, 0, ct, :],
                                               ALU.mult,
                                               ALU.add).then_inc(S("s_cva"), 1)
                for tile in range(2):
                    gi = uc * 2 + tile
                    v.wait_ge(S("s_gmm"), gi + 1)
                    if gi >= 2:
                        v.wait_ge(S(f"s_gst{gi % 2}"), 16 * (gi // 2))
                    v.tensor_copy(gev[:, gi % 2, :], psC[:, 0, :])
                    v.tensor_copy(dtraw[:, gi, :],
                                  psC[:, 1, 0:8]).then_inc(S("s_gev"), 1)
                for tile in range(2):
                    c = uc * 2 + tile
                    for ct in range(4):
                        tx = c * 4 + ct
                        v.wait_ge(S("s_trx"), tx + 1)
                        if c >= 2 and ct == 0:
                            v.wait_ge(S(f"s_xst{c % 2}"), 16 * (c // 2))
                        srcp = pTb[:, 0:128] if tx % 2 == 0 else pTd[:, 0:128]
                        v.tensor_copy(xs_ev[:, c % 2, ct * 128:(ct + 1) * 128],
                                      srcp).then_inc(S("s_xse"), 1)
                for tile in range(2):
                    c = uc * 2 + tile
                    v.wait_ge(S("s_cvs"), 5 * (uc + 1))
                    v.tensor_copy(B_fm[:, c * 128:(c + 1) * 128],
                                  cvo[0:64, uc % 2, 4, tile * 128:(tile + 1) * 128]
                                  ).then_inc(S("s_bce"), 1)
            v.wait_ge(S("s_gev"), 2 * NUC)
            v.wait_ge(S("s_tin"), 3)
            v.tensor_tensor(lgs1[:], dtraw[:],
                            dtb_bc[:].unsqueeze(1).broadcast_to([128, NCH, 8]),
                            ALU.add).then_inc(S("s_dte"), 1)
            v.wait_ge(S("s_dte"), 1)
            v.tensor_scalar(lgs2[:].opt(), lgs1[:].opt(), -1.0, None,
                            ALU.mult).then_inc(S("s_dte"), 1)
            v.wait_ge(S("s_dte"), 2)
            v.tensor_tensor(lgs2[:].opt(), lgs2[:].opt(), lgs1[:].opt(),
                            ALU.max).then_inc(S("s_dte"), 1)
            v.wait_ge(S("s_dte"), 5)
            v.tensor_scalar(dt_tm[:].opt(), lgs1[:].opt(), 0.0, None,
                            ALU.max).then_inc(S("s_dte"), 1)
            v.wait_ge(S("s_dte"), 6)
            v.tensor_tensor(dt_tm[:].opt(), dt_tm[:].opt(),
                            lgs2[:].opt(), ALU.add).then_inc(S("s_dte"), 1)
            v.wait_ge(S("s_dte"), 7)
            v.tensor_tensor(dA_tm[:], dt_tm[:],
                            A_bc[:].unsqueeze(1).broadcast_to([128, NCH, 8]),
                            ALU.mult).then_inc(S("s_dte"), 1)
            for c in range(NCH):
                v.wait_ge(S("s_asm"), c + 1)
                v.tensor_copy(AS_tm[:, c, :],
                              psA[:, c % 2, 0:16]).then_inc(S("s_ase"), 1)
            v.wait_ge(S("s_ase"), NCH)
            v.tensor_tensor(lgs1[:], AS_tm[:, :, 0:8], AS_tm[:, :, 8:16],
                            ALU.add).then_inc(S("s_bx"), 1)
            v.wait_ge(S("s_bx"), 4)
            for c in range(NCH):
                v.wait_ge(S("s_xse"), 4 * (c + 1))
                if c >= 2:
                    v.wait_ge(S("s_yim"), 8 * (c - 1))
                    v.wait_ge(S("s_smm"), c - 1)
                v.wait_ge(S(f"s_xrl{c % 2}"), 16 * (c // 2 + 1))
                v.tensor_tensor(xdt[:, c % 2, :].rearrange("p (h d) -> p h d", d=64),
                                xrl[:, c % 2, :].rearrange("p (h d) -> p h d", d=64),
                                dt_tm[:, c, :].unsqueeze(2)
                                .broadcast_to([128, 8, 64]),
                                ALU.mult).then_inc(S("s_xde"), 1)
                v.wait_ge(S("s_xde"), 2 * c + 1)
                v.tensor_tensor(xdd[:, c % 2, :].rearrange("p (h d) -> p h d", d=64),
                                xdt[:, c % 2, :].rearrange("p (h d) -> p h d", d=64),
                                eS_all[:, c, :].unsqueeze(2)
                                .broadcast_to([128, 8, 64]),
                                ALU.mult).then_inc(S("s_xde"), 1)
                v.wait_ge(S("s_trp"), c + 1)
                if c >= 2:
                    v.wait_ge(S("s_smm"), c - 1)
                v.tensor_copy(btb[:, c % 2, :], pTd[:, 256:320]).then_inc(S("s_bte"), 1)
                for h in range(8):
                    u = c * 8 + h
                    v.wait_ge(S("s_gib"), u + 1)
                    if u >= 2:
                        v.wait_ge(S("s_dex"), u - 1)
                    v.scalar_tensor_tensor(dlg[:, u % 2, :], psC[:, 1, 0:128],
                                           AS_tm[:, c, h:h + 1], mska[:],
                                           ALU.subtract, ALU.add
                                           ).then_inc(S("s_dlg"), 1)
                    v.wait_ge(S("s_dex"), u + 1)
                    v.wait_ge(S("s_cbt"), c + 1)
                    if u >= 1:
                        v.wait_ge(S("s_yim"), u)
                    v.tensor_tensor(attb[:, u % 2, :], psC[:, 0, 0:128],
                                    dexb[:, u % 2, :],
                                    ALU.mult).then_inc(S("s_att"), 1)
                v.wait_ge(S("s_yim"), 8 * (c + 1))
                v.wait_ge(S("s_yev"), 3 * c + 1)
                v.tensor_tensor(yt2[:].rearrange("p (h d) -> p h d", d=64),
                                psB[:].rearrange("p (h d) -> p h d", d=64),
                                eG_all[:, c, :].unsqueeze(2)
                                .broadcast_to([128, 8, 64]),
                                ALU.mult).then_inc(S("s_vy"), 1)
                v.wait_ge(S("s_vy"), c + 1)
                v.tensor_tensor(ysb[:], ysb[:], yt2[:],
                                ALU.add).then_inc(S("s_yev"), 1)
                v.wait_ge(S("s_yev"), 3 * c + 3)
                v.wait_ge(S("s_sgl"), c + 1)
                if c >= 2:
                    v.wait_ge(S(f"s_ost{c % 2}"), 16 * (c // 2))
                v.tensor_tensor(yob[:, c % 2, :], ysb[:],
                                sgb[:, c % 2, :], ALU.mult).then_inc(S("s_yo"), 1)
                v.wait_ge(S("s_smm"), c + 1)
                v.tensor_tensor(P_sb[:].rearrange("n h d -> n h d"), P_sb[:],
                                dec_all[0:64, c, :].unsqueeze(2)
                                .broadcast_to([64, 8, 64]),
                                ALU.mult).then_inc(S("s_pst"), 1)
                v.wait_ge(S("s_pst"), 2 * c + 1)
                v.tensor_tensor(P_sb[:], P_sb[:],
                                psE[:].rearrange("n (h d) -> n h d", d=64),
                                ALU.add).then_inc(S("s_pst"), 1)
                v.wait_ge(S("s_pst"), 2 * c + 2)
                v.wait_ge(S("s_yim"), 8 * (c + 1))
                v.tensor_copy(P_bf[:], P_sb[:]).then_inc(S("s_pbf"), 1)

        @blk.gpsimd
        def _(g_):
            g_.wait_ge(S("s_gsv"), 1)
            for uc in range(NUC):
                for ct in range(2, 3):
                    base = 7 * uc + 7 * (ct - 2)
                    g_.wait_ge(S("s_pev"), 5 * (uc + 1))
                    g_.wait_ge(S("s_cvh"), uc + 1)
                    if uc >= 1:
                        g_.wait_ge(S("s_cvs"), 5 * uc)
                    g_.tensor_scalar(cacc[:, 0, ct, :],
                                     raw[:, uc % 2, ct, 0:256],
                                     wcv[:, ct, 0:1], None,
                                     ALU.mult).then_inc(S("s_cvb"), 1)
                    nn_ = base + 1
                    for tau in range(1, 4):
                        g_.tensor_scalar(ctmp[:],
                                         raw[:, uc % 2, ct, tau:tau + 256],
                                         wcv[:, ct, tau:tau + 1], None,
                                         ALU.mult).then_inc(S("s_cvb"), 1)
                        nn_ += 1
                        g_.wait_ge(S("s_cvb"), nn_)
                        g_.tensor_tensor(cacc[:, 0, ct, :], cacc[:, 0, ct, :],
                                         ctmp[:], ALU.add).then_inc(S("s_cvb"), 1)
                        nn_ += 1
                        g_.wait_ge(S("s_cvb"), nn_)
                for tile in range(2):
                    gi = uc * 2 + tile
                    g_.wait_ge(S("s_gev"), gi + 1)
                    g_.dma_start(out=g_dram[gi],
                                 in_=gev[:, gi % 2, :]).then_inc(
                                     S(f"s_gst{gi % 2}"), 16)
                    g_.wait_ge(S("s_xse"), 4 * (gi + 1))
                    g_.dma_start(out=xs_dram[gi],
                                 in_=xs_ev[:, gi % 2, :]).then_inc(
                                     S(f"s_xst{gi % 2}"), 16)
            g_.wait_ge(S("s_dte"), 8)
            g_.wait_ge(S("s_bx"), 4)
            for c in range(NCH):
                for h in range(8):
                    u = c * 8 + h
                    if u >= 2:
                        g_.wait_ge(S("s_gib"), u - 1)
                    g_.tensor_scalar(rsD[:, u % 2, :], tri[:],
                                     dA_tm[:, c, h:h + 1], None,
                                     ALU.mult).then_inc(S("s_rsd"), 1)
                g_.wait_ge(S("s_yev"), 3 * c + 2)
                g_.wait_ge(S(f"s_xrl{c % 2}"), 16 * (c // 2 + 1))
                g_.tensor_tensor(yt3[:].rearrange("p (h d) -> p h d", d=64),
                                 xrl[:, c % 2, :].rearrange("p (h d) -> p h d", d=64),
                                 D_bc[:].unsqueeze(2).broadcast_to([128, 8, 64]),
                                 ALU.mult).then_inc(S("s_gy"), 1)
                g_.wait_ge(S("s_gy"), c + 1)
                g_.tensor_tensor(ysb[:], ysb[:], yt3[:],
                                 ALU.add).then_inc(S("s_yev"), 1)
                g_.wait_ge(S("s_yo"), c + 1)
                g_.dma_start(out=y_out[c],
                             in_=yob[:, c % 2, :]).then_inc(S(f"s_ost{c % 2}"), 16)


def build_c2():
    NCH, NUC = 64, 32
    nc = bass.Bass("TRN2", target_bir_lowering=False)
    ext = {}
    ext["u_in"] = nc.dram_tensor("u_in", [NUC, 128, 16, 256], BF16, kind="ExternalInput").ap()
    ext["wxs_in"] = nc.dram_tensor("wxs_in", [5, 128, 16, 128], F32, kind="ExternalInput").ap()
    ext["wg_in"] = nc.dram_tensor("wg_in", [16, 128, 512], F32, kind="ExternalInput").ap()
    ext["wdt_in"] = nc.dram_tensor("wdt_in", [16, 128, 8], F32, kind="ExternalInput").ap()
    ext["wcv_in"] = nc.dram_tensor("wcv_in", [128, 5, 4], F32, kind="ExternalInput").ap()
    ext["bcv_in"] = nc.dram_tensor("bcv_in", [128, 5], F32, kind="ExternalInput").ap()
    ext["dtb_in"] = nc.dram_tensor("dtb_in", [1, 8], F32, kind="ExternalInput").ap()
    ext["alog_in"] = nc.dram_tensor("alog_in", [1, 8], F32, kind="ExternalInput").ap()
    ext["dd_in"] = nc.dram_tensor("dd_in", [1, 8], F32, kind="ExternalInput").ap()
    ext["tri_in"] = nc.dram_tensor("tri_in", [128, 128], F32, kind="ExternalInput").ap()
    ext["suf_in"] = nc.dram_tensor("suf_in", [128, 128], F32, kind="ExternalInput").ap()
    ext["one_in"] = nc.dram_tensor("one_in", [128, 128], F32, kind="ExternalInput").ap()
    ext["msk_in"] = nc.dram_tensor("msk_in", [128, 128], F32, kind="ExternalInput").ap()
    ext["id_in"] = nc.dram_tensor("id_in", [128, 128], F32, kind="ExternalInput").ap()
    ext["y_out"] = nc.dram_tensor("y_out", [NCH, 128, 512], BF16, kind="ExternalOutput").ap()
    ext["g_dram"] = nc.dram_tensor("g_dram", [NCH, 128, 512], BF16).ap()
    ext["xs_dram"] = nc.dram_tensor("xs_dram", [NCH, 128, 512], BF16).ap()
    emit_c2(nc, ext)
    return nc



def run_c2(u_full, in_proj_w, conv_w, conv_b, dt_bias, A_log, Dp, core_heads=8):
    nc = _prog("c2", build_c2)
    u_r = np.ascontiguousarray(
        u_full.reshape(16, 128, 32, 256).transpose(2, 1, 0, 3))
    ident = np.eye(128, dtype=np.float32)
    tri = np.ascontiguousarray(np.tril(np.ones((128, 128), np.float32), 0).T)
    suf = np.ascontiguousarray(1.0 - tri)
    onesf = np.ones((128, 128), np.float32)
    mska = np.where(np.arange(128)[None, :] >= np.arange(128)[:, None],
                    np.float32(0.0), np.float32(-1e30)).astype(np.float32)
    in_maps = []
    for c in range(NC):
        xs_rows = np.arange(4096 + 512 * c, 4096 + 512 * (c + 1))
        bc_rows = np.arange(8192, 8320)
        rows = np.concatenate([xs_rows, bc_rows])
        W = _f32(in_proj_w[rows])
        wxs = np.ascontiguousarray(
            W.T.reshape(16, 128, 5, 128).transpose(2, 1, 0, 3))
        wg = np.ascontiguousarray(
            _f32(in_proj_w[512 * c:512 * (c + 1)]).T.reshape(16, 128, 512))
        wdt = np.ascontiguousarray(
            _f32(in_proj_w[8320 + 8 * c:8320 + 8 * (c + 1)]).T.reshape(16, 128, 8))
        ch = np.concatenate([np.arange(512 * c, 512 * (c + 1)),
                             np.arange(4096, 4224)])
        wcv = np.ascontiguousarray(
            _f32(conv_w[ch]).reshape(5, 128, 4).transpose(1, 0, 2))
        bcv = np.ascontiguousarray(
            _f32(conv_b[ch]).reshape(5, 128).T)
        in_maps.append({
            "u_in": u_r, "wxs_in": wxs, "wg_in": wg, "wdt_in": wdt,
            "wcv_in": wcv, "bcv_in": bcv,
            "dtb_in": _f32(dt_bias[8 * c:8 * (c + 1)]).reshape(1, 8),
            "alog_in": _f32(A_log[8 * c:8 * (c + 1)]).reshape(1, 8),
            "dd_in": _f32(Dp[8 * c:8 * (c + 1)]).reshape(1, 8),
            "tri_in": tri, "suf_in": suf, "one_in": onesf, "msk_in": mska,
            "id_in": ident,
        })
    res = run_bass_kernel_spmd(nc, in_maps, core_ids=CORE_IDS)
    return np.concatenate(
        [r["y_out"].reshape(T, 512) for r in res.results], axis=1)



def emit_tprep(nc, ext, tag="", fused=False):
    sb = lambda n, s, d: nc.sbuf_tensor(tag + n, s, d)
    ps = lambda n, s, d: nc.psum_tensor(tag + n, s, d)
    w1s_in = ext["w1s_in"]
    w2s_in = ext["w2s_in"]
    ones_in = ext["ones_in"]
    t1o = ext["t1o"]
    t2o = ext["t2o"]
    ccin_t = ext["ccin_t"]
    ccout_t = ext["ccout_t"]
    t1f = ext.get("t1f")
    t2f = ext.get("t2f")
    cc_in = ccin_t.ap()
    cc_out = ccout_t.ap()

    with ExitStack() as ctx:
        ent = ctx.enter_context
        wsl = ent(sb("wsl", [128, 16, 2048], F32))
        mst = ent(sb("mst", [128, 2048], F32))
        tern = ent(sb("tern", [128, 2, 2048], BF16))
        racc = ent(sb("racc", [128, 8], F32))
        onec = ent(sb("onec", [128, 128], F32))
        thrc = ent(sb("thrc", [128, 4], F32))
        ccs = ent(sb("ccs", [1, 16], F32))
        pr = ent(ps("pr", [128, 512], F32))

        s_l1 = ent(nc.semaphore("s_l1"))
        s_l2 = ent(nc.semaphore("s_l2"))
        s_on = ent(nc.semaphore("s_on"))
        s_r = ent(nc.semaphore("s_r"))
        s_mm = ent(nc.semaphore("s_mm"))
        s_cc = ent(nc.semaphore("s_cc"))
        s_th = ent(nc.semaphore("s_th"))
        s_cr = ent(nc.semaphore("s_cr"))
        s_tn = ent(nc.semaphore("s_tn"))
        s_to = [ent(nc.semaphore(f"s_to{i}")) for i in range(2)]
        s_ag = ent(nc.semaphore("s_ag"))
        blk = ent(nc.Block())

        @blk.sync
        def _(q):
            q.dma_start(out=onec[:], in_=ones_in[:]).then_inc(s_on, 16)
            for b in range(8):
                q.dma_start(out=wsl[:, b, :], in_=w1s_in[b]).then_inc(s_l1, 16)
            for b in range(8):
                q.dma_start(out=wsl[:, 8 + b, :], in_=w2s_in[b]).then_inc(s_l2, 16)
            q.wait_ge(s_mm, 2)
            q.dma_start(out=cc_in[:], in_=ccs[:]).then_inc(s_cc, 16)
            for b in range(16):
                q.wait_ge(s_tn, 2 * (b + 1))
                dst = t1o[b] if b < 8 else t2o[b - 8]
                q.dma_start(out=dst, in_=tern[:, b % 2, :]
                            ).then_inc(s_to[b % 2], 16)
            q.wait_ge(s_to[0], 16 * 8)
            q.wait_ge(s_to[1], 16 * 8)

        @blk.scalar
        def _(s):
            s.wait_ge(s_cr, 16)
            s.activation(ccs[:, 2:4], ccs[:, 0:2], AF.Copy,
                         scale=0.7 / (DM * IF)).then_inc(s_th, 1)

        @blk.tensor
        def _(t):
            t.wait_ge(s_r, 33)
            t.wait_ge(s_on, 16)
            t.matmul(pr[0:1, 0:2], onec[:, 0:1], racc[:, 0:2],
                     start=True, stop=True).then_inc(s_mm, 1)
            t.wait_ge(s_th, 1)
            t.matmul(pr[0:128, 4:6], onec[0:1, :], ccs[0:1, 2:4],
                     start=True, stop=True,
                     skip_group_check=True).then_inc(s_mm, 1)

        @blk.vector
        def _(v):
            v.wait_ge(s_l1, 16 * 8)
            v.memset(racc[:], 0.0).then_inc(s_r, 1)
            n = 1
            v.wait_ge(s_r, n)
            for b in range(16):
                if b == 8:
                    v.wait_ge(s_l2, 16 * 8)
                j = 0 if b < 8 else 1
                v.reduce_sum(racc[:, 2 + j:3 + j], wsl[:, b, :], axis=AX.X,
                             apply_absolute_value=True).then_inc(s_r, 1)
                n += 1
                v.wait_ge(s_r, n)
                v.tensor_tensor(racc[:, j:j + 1], racc[:, j:j + 1],
                                racc[:, 2 + j:3 + j], ALU.add).then_inc(s_r, 1)
                n += 1
                v.wait_ge(s_r, n)
            v.wait_ge(s_mm, 1)
            v.memset(ccs[:], 0.0).then_inc(s_r, 1)
            n += 1
            v.wait_ge(s_r, n)
            v.tensor_copy(ccs[:, 0:2], pr[0:1, 0:2]).then_inc(s_mm, 1)
            v.wait_ge(s_mm, 3)
            v.tensor_copy(thrc[:, 0:2], pr[:, 4:6]).then_inc(s_r, 1)
            n += 1
            v.wait_ge(s_r, n)
            v.tensor_scalar(thrc[:, 2:4], thrc[:, 0:2], -1.0, None,
                            ALU.mult).then_inc(s_r, 1)
            n += 1
            v.wait_ge(s_r, n)
            for b in range(16):
                j = 0 if b < 8 else 1
                v.tensor_scalar(mst[:], wsl[:, b, :], thrc[:, 2 + j:3 + j],
                                None, ALU.is_le).then_inc(s_tn, 1)
                v.wait_ge(s_tn, 2 * b + 1)
                if b >= 2:
                    v.wait_ge(s_to[b % 2], 16 * (b // 2))
                v.scalar_tensor_tensor(tern[:, b % 2, :], wsl[:, b, :],
                                       thrc[:, j:j + 1], mst[:],
                                       ALU.is_ge, ALU.subtract
                                       ).then_inc(s_tn, 1)
                v.wait_ge(s_tn, 2 * b + 2)

        @blk.gpsimd
        def _(g_):
            g_.wait_ge(s_mm, 2)
            g_.wait_ge(s_cc, 16)
            g_.collective_compute(
                "AllReduce", ALU.add, replica_groups=[CORE_IDS],
                ins=[cc_in.opt()], outs=[cc_out.opt()]).then_inc(s_cc, 1)
            g_.wait_ge(s_cc, 17)
            g_.dma_start(out=ccs[:, 0:2], in_=cc_out[:, 0:2]).then_inc(s_cr, 16)
            if fused:
                g_.wait_ge(s_to[0], 16 * 8)
                g_.wait_ge(s_to[1], 16 * 8)
                g_.collective_compute(
                    "AllGather", ALU.bypass, replica_groups=[CORE_IDS],
                    ins=[t1o.opt()], outs=[t1f.opt()]).then_inc(s_ag, 1)
                g_.collective_compute(
                    "AllGather", ALU.bypass, replica_groups=[CORE_IDS],
                    ins=[t2o.opt()], outs=[t2f.opt()]).then_inc(s_ag, 1)
                g_.wait_ge(s_ag, 2)


def build_tprep():
    nc = bass.Bass("TRN2", target_bir_lowering=False)
    ext = {}
    ext["w1s_in"] = nc.dram_tensor("w1s_in", [8, 128, 2048], F32, kind="ExternalInput").ap()
    ext["w2s_in"] = nc.dram_tensor("w2s_in", [8, 128, 2048], F32, kind="ExternalInput").ap()
    ext["ones_in"] = nc.dram_tensor("ones_in", [128, 128], F32, kind="ExternalInput").ap()
    ext["t1o"] = nc.dram_tensor("t1o", [8, 128, 2048], BF16, kind="ExternalOutput").ap()
    ext["t2o"] = nc.dram_tensor("t2o", [8, 128, 2048], BF16, kind="ExternalOutput").ap()
    ext["ccin_t"] = nc.dram_tensor("cc_in", [1, 16], F32)
    ext["ccout_t"] = nc.dram_tensor("cc_out", [1, 16], F32)
    emit_tprep(nc, ext)
    return nc



def _ffn1_wlayout(w1, w2):
    w1L = np.ascontiguousarray(
        w1.T.reshape(16, 128, 32, 256).transpose(2, 1, 0, 3))
    w2L = np.ascontiguousarray(
        w2.T.reshape(64, 128, 16, 128).transpose(2, 1, 0, 3))
    b1_ = w1L.reshape(32, 128, 2, 2048).transpose(0, 2, 1, 3).reshape(64, 128, 2048)
    b2_ = w2L.reshape(16, 128, 4, 2048).transpose(0, 2, 1, 3).reshape(64, 128, 2048)
    return b1_, b2_


def run_tprep(w1, w2):
    nc = _prog("tprep", build_tprep)
    b1_, b2_ = _ffn1_wlayout(_f32(w1), _f32(w2))
    ones = np.ones((128, 128), dtype=np.float32)
    in_maps = []
    for c in range(NC):
        in_maps.append({
            "w1s_in": np.ascontiguousarray(b1_[8 * c:8 * c + 8]),
            "w2s_in": np.ascontiguousarray(b2_[8 * c:8 * c + 8]),
            "ones_in": ones,
        })
    res = run_bass_kernel_spmd(nc, in_maps, core_ids=CORE_IDS)
    t1b = np.concatenate([r["t1o"] for r in res.results], axis=0)
    t2b = np.concatenate([r["t2o"] for r in res.results], axis=0)
    t1 = t1b.reshape(32, 2, 128, 2048).transpose(0, 2, 1, 3).reshape(32, 128, 16, 256)
    t2 = t2b.reshape(16, 4, 128, 2048).transpose(0, 2, 1, 3).reshape(16, 128, 64, 128)
    return np.ascontiguousarray(t1), np.ascontiguousarray(t2)



def emit_ffn1(nc, ext, tag="", prefill=True, fused=False):
    sb = lambda n, s, d: nc.sbuf_tensor(tag + n, s, d)
    ps = lambda n, s, d: nc.psum_tensor(tag + n, s, d)
    x_in = ext["x_in"]
    t1_in = ext.get("t1_in")
    t2_in = ext.get("t2_in")
    t1f = ext.get("t1f")
    t2f = ext.get("t2f")
    b1c_in = ext["b1c_in"]
    b2_in = ext["b2_in"]
    id_in = ext["id_in"]
    x_out = ext["x_out"]

    NT = TL // 128
    NK = DM // 128
    NQ = IF // 128
    NH = 2
    GRP = 2
    NG = NQ // GRP

    with ExitStack() as ctx:
        ent = ctx.enter_context
        u_fm = ent(sb("u_fm", [128, NK, TL], BF16))
        u_tm = ent(sb("u_tm", [128, DM], BF16))
        xt = ent(sb("xt", [128, 2, DM], F32))
        interm = ent(sb("interm", [128, NQ, 512], BF16))
        wbuf = ent(sb("wbuf", [128, 2, NK, 256], BF16))
        wdb = ent(sb("wdb", [128, 2, NQ, 128], BF16))
        bcol = ent(sb("bcol", [128, 64], F32))
        b2r = ent(sb("b2r", [1, DM], F32))
        b2h = ent(sb("b2h", [128, DM], F32))
        onesr = ent(sb("onesr", [1, 128], F32))
        cidf = ent(sb("cidf", [128, 128], F32))
        idb = ent(sb("idb", [128, 128], BF16))
        dfm = ent(sb("dfm", [128, 2, 512], BF16))
        xev = ent(sb("xev", [128, 2, 128], F32))
        cst = ent(sb("cst", [128, 8], F32))

        pt = ent(ps("pt", [128, 2, 1024], BF16))
        pu = ent(ps("pu", [128, 2, 512], F32))
        pd = ent(ps("pd", [128, 2, 512], F32))

        s_cid = ent(nc.semaphore("s_cid"))
        s_cv = ent(nc.semaphore("s_cv"))
        s_pc = ent(nc.semaphore("s_pc"))
        s_xl = [ent(nc.semaphore(f"s_xl{i}")) for i in range(2)]
        s_st = ent(nc.semaphore("s_st"))
        s_sc = ent(nc.semaphore("s_sc"))
        s_u = ent(nc.semaphore("s_u"))
        s_tp = ent(nc.semaphore("s_tp"))
        s_ue = ent(nc.semaphore("s_ue"))
        s_wl = [ent(nc.semaphore(f"s_wl{i}")) for i in range(2)]
        s_gp = ent(nc.semaphore("s_gp"))
        s_ep = ent(nc.semaphore("s_ep"))
        s_dl = [ent(nc.semaphore(f"s_dl{i}")) for i in range(2)]
        s_dp = ent(nc.semaphore("s_dp"))
        s_df = ent(nc.semaphore("s_df"))
        s_dt = ent(nc.semaphore("s_dt"))
        s_xv = ent(nc.semaphore("s_xv"))
        s_b2 = ent(nc.semaphore("s_b2"))
        s_b2m = ent(nc.semaphore("s_b2m"))
        s_so = [ent(nc.semaphore(f"s_so{i}")) for i in range(2)]
        blk = ent(nc.Block())

        @blk.sync
        def _(q):
            if prefill:
                q.dma_start(out=x_out[:, :], in_=x_in[:, :]).then_inc(s_pc, 16)
            q.dma_start(out=cidf[:], in_=id_in[:]).then_inc(s_cid, 16)
            q.dma_start(out=bcol[:], in_=b1c_in[:]).then_inc(s_cid, 16)
            q.dma_start(out=b2r[:], in_=b2_in[:]).then_inc(s_cid, 16)
            for t in range(NT):
                if t >= 2:
                    q.wait_ge(s_u, t - 1)
                q.dma_start(out=xt[:, t % 2, :],
                            in_=x_in[t * 128:(t + 1) * 128, :]
                            ).then_inc(s_xl[t % 2], 16)
            if prefill:
                q.wait_ge(s_pc, 16)
            for h in range(NH):
                for g in range(NG):
                    gi = h * NG + g
                    if gi >= 2:
                        q.wait_ge(s_gp, GRP * (gi - 1))
                    if fused:
                        q.dma_start(out=wbuf[:, gi % 2, :, :].rearrange(
                                        "p (h k) c -> p h (k c)", h=2),
                                    in_=t1f[2 * g:2 * g + 2].rearrange(
                                        "h p f -> p h f")
                                    ).then_inc(s_wl[gi % 2], 16)
                    else:
                        q.dma_start(out=wbuf[:, gi % 2, :, :],
                                    in_=t1_in[g]).then_inc(s_wl[gi % 2], 16)
            q.wait_ge(s_so[0], NH * 32 * 16)
            q.wait_ge(s_so[1], NH * 32 * 16)

        @blk.scalar
        def _(s):
            def w2load(D):
                if fused:
                    dm = D % 16
                    return s.dma_start(
                        out=wdb[:, D % 2, :, :].rearrange(
                            "p (qx k) c -> p qx (k c)", qx=4),
                        in_=t2f[4 * dm:4 * dm + 4].rearrange("q p f -> p q f"))
                return s.dma_start(out=wdb[:, D % 2, :, :], in_=t2_in[D % 16])
            w2load(0).then_inc(s_dl[0], 16)
            w2load(1).then_inc(s_dl[1], 16)
            n_sc = 0
            for t in range(NT):
                s.wait_ge(s_xl[t % 2], 16 * (t // 2 + 1))
                if t >= 1:
                    s.wait_ge(s_u, t)
                    s.wait_ge(s_tp, 16 * t)
                xa = xt[:, t % 2, :]
                n_sc = emit_ln_stats(s, xa, u_tm[:], cst, LNEPS, s_sc, n_sc)
                s.nop().then_inc(s_st, 1)
            for h in range(NH):
                for qh in range(NQ):
                    Q = h * NQ + qh
                    s.wait_ge(s_gp, Q + 1)
                    if Q >= 2:
                        s.wait_ge(s_ep, Q - 1)
                    if qh == 0 and h == 1:
                        s.wait_ge(s_dp, 16)
                    s.activation(interm[:, qh, :], pu[:, qh % 2, :], AF.Gelu,
                                 bias=bcol[:, qh:qh + 1]).then_inc(s_ep, 1)
                for dm in range(16):
                    D = h * 16 + dm
                    if D + 2 < 32:
                        s.wait_ge(s_dp, D + 1)
                        w2load(D + 2).then_inc(s_dl[D % 2], 16)

        @blk.tensor
        def _(t):
            t.wait_ge(s_cid, 48)
            t.wait_ge(s_cv, 2)
            for j in range(4):
                if j >= 2:
                    t.wait_ge(s_b2, j - 1)
                t.matmul(pd[:, j % 2, :], onesr[:],
                         b2r[:, j * 512:(j + 1) * 512], start=True, stop=True,
                         skip_group_check=True).then_inc(s_b2m, 1)
            for tt_ in range(NT):
                t.wait_ge(s_u, tt_ + 1)
                for c in range(NK):
                    g = tt_ * NK + c
                    if g >= 2:
                        t.wait_ge(s_ue, g - 1)
                    t.transpose(pt[:, g % 2, 0:128], u_tm[:, c * 128:(c + 1) * 128],
                                idb[:]).then_inc(s_tp, 1)
            for h in range(NH):
                t.wait_ge(s_ue, NK * 4 * (h + 1))
                for qh in range(NQ):
                    Q = h * NQ + qh
                    gi = h * NG + qh // GRP
                    cc = qh % GRP
                    t.wait_ge(s_wl[gi % 2], 16 * (gi // 2 + 1))
                    if Q >= 2:
                        t.wait_ge(s_ep, Q - 1)
                    for k in range(NK):
                        ins = t.matmul(pu[:, qh % 2, :],
                                       wbuf[:, gi % 2, k, cc * 128:(cc + 1) * 128],
                                       u_fm[:, k, h * 512:(h + 1) * 512],
                                       start=(k == 0), stop=(k == NK - 1))
                    ins.then_inc(s_gp, 1)
                for dm in range(16):
                    D = h * 16 + dm
                    t.wait_ge(s_dl[D % 2], 16 * (D // 2 + 1))
                    if dm == 0:
                        t.wait_ge(s_ep, (h + 1) * NQ)
                    if D >= 2:
                        t.wait_ge(s_df, D - 1)
                    for qc in range(NQ):
                        ins = t.matmul(pd[:, D % 2, :], wdb[:, D % 2, qc, :],
                                       interm[:, qc, :],
                                       start=(qc == 0), stop=(qc == NQ - 1))
                    ins.then_inc(s_dp, 1)
                    if dm >= 1:
                        Dp = D - 1
                        t.wait_ge(s_df, Dp + 1)
                        for sub in range(4):
                            B = Dp * 4 + sub
                            if B >= 2:
                                t.wait_ge(s_xv, B - 1)
                            t.transpose(pt[:, B % 2, 0:128],
                                        dfm[:, Dp % 2, sub * 128:(sub + 1) * 128],
                                        idb[:]).then_inc(s_dt, 1)
                D = h * 16 + 15
                t.wait_ge(s_df, D + 1)
                for sub in range(4):
                    B = D * 4 + sub
                    t.wait_ge(s_xv, B - 1)
                    t.transpose(pt[:, B % 2, 0:128],
                                dfm[:, D % 2, sub * 128:(sub + 1) * 128],
                                idb[:]).then_inc(s_dt, 1)

        @blk.vector
        def _(v):
            v.wait_ge(s_cid, 48)
            v.tensor_copy(idb[:], cidf[:]).then_inc(s_cv, 1)
            v.memset(onesr[:], 1.0).then_inc(s_cv, 1)
            for j in range(4):
                v.wait_ge(s_b2m, j + 1)
                v.tensor_scalar(b2h[:, j * 512:(j + 1) * 512], pd[:, j % 2, :],
                                0.5, None, ALU.mult).then_inc(s_b2, 1)
            v.wait_ge(s_b2, 4)
            for tt_ in range(NT):
                v.wait_ge(s_st, tt_ + 1)
                if tt_ >= 1:
                    v.wait_ge(s_tp, 16 * tt_)
                v.scalar_tensor_tensor(
                    u_tm[:], xt[:, tt_ % 2, :], cst[:, 2:3],
                    cst[:, 5:6].broadcast_to([128, DM]),
                    ALU.subtract, ALU.mult).then_inc(s_u, 1)
                for c in range(NK):
                    g = tt_ * NK + c
                    v.wait_ge(s_tp, g + 1)
                    v.tensor_copy(u_fm[:, c, tt_ * 128:(tt_ + 1) * 128],
                                  pt[:, g % 2, 0:128]).then_inc(s_ue, 1)
            for h in range(NH):
                if h == 1:
                    v.wait_ge(s_dp, 16)
                for dm in range(16):
                    D = h * 16 + dm
                    v.wait_ge(s_dp, D + 1)
                    if D >= 2:
                        v.wait_ge(s_dt, 4 * (D - 1))
                    v.tensor_copy(dfm[:, D % 2, :],
                                  pd[:, D % 2, :]).then_inc(s_df, 1)
                    if dm >= 1:
                        Dp = D - 1
                        dmp = Dp % 16
                        for sub in range(4):
                            B = Dp * 4 + sub
                            v.wait_ge(s_dt, B + 1)
                            if B >= 2:
                                v.wait_ge(s_so[B % 2], 16 * (B // 2))
                            v.scalar_tensor_tensor(
                                xev[:, B % 2, :], pt[:, B % 2, 0:128], 0.5,
                                b2h[:, dmp * 128:(dmp + 1) * 128],
                                ALU.mult, ALU.add).then_inc(s_xv, 1)
                D = h * 16 + 15
                dmp = 15
                for sub in range(4):
                    B = D * 4 + sub
                    v.wait_ge(s_dt, B + 1)
                    v.wait_ge(s_so[B % 2], 16 * (B // 2))
                    v.scalar_tensor_tensor(
                        xev[:, B % 2, :], pt[:, B % 2, 0:128], 0.5,
                        b2h[:, dmp * 128:(dmp + 1) * 128],
                        ALU.mult, ALU.add).then_inc(s_xv, 1)

        @blk.gpsimd
        def _(g_):
            if prefill:
                g_.wait_ge(s_pc, 16)
            else:
                g_.wait_ge(s_u, NT)
            for h in range(NH):
                for dm in range(16):
                    D = h * 16 + dm
                    for sub in range(4):
                        B = D * 4 + sub
                        g_.wait_ge(s_xv, B + 1)
                        g_.dma_start(
                            out=x_out[h * 512 + sub * 128:
                                      h * 512 + (sub + 1) * 128,
                                      dm * 128:(dm + 1) * 128],
                            in_=xev[:, B % 2, :],
                            accum_op=ALU.add).then_inc(s_so[B % 2], 16)


def build_ffn1():
    nc = bass.Bass("TRN2", target_bir_lowering=False)
    ext = {}
    ext["x_in"] = nc.dram_tensor("x_in", [TL, DM], F32, kind="ExternalInput").ap()
    ext["t1_in"] = nc.dram_tensor("t1_in", [32, 128, 16, 256], BF16, kind="ExternalInput").ap()
    ext["t2_in"] = nc.dram_tensor("t2_in", [16, 128, 64, 128], BF16, kind="ExternalInput").ap()
    ext["b1c_in"] = nc.dram_tensor("b1c_in", [128, 64], F32, kind="ExternalInput").ap()
    ext["b2_in"] = nc.dram_tensor("b2_in", [1, DM], F32, kind="ExternalInput").ap()
    ext["id_in"] = nc.dram_tensor("id_in", [128, 128], F32, kind="ExternalInput").ap()
    ext["x_out"] = nc.dram_tensor("x_out", [TL, DM], F32, kind="ExternalOutput").ap()
    emit_ffn1(nc, ext)
    return nc



def run_ffn1(x, t1, t2, b1, b2):
    nc = _prog("ffn1", build_ffn1)
    ident = np.eye(128, dtype=np.float32)
    b1c = np.ascontiguousarray(_f32(b1).reshape(64, 128).T)
    b2r = _f32(b2).reshape(1, DM)
    in_maps = []
    for c in range(NC):
        in_maps.append({
            "x_in": _f32(x[c * TL:(c + 1) * TL]),
            "t1_in": t1, "t2_in": t2,
            "b1c_in": b1c, "b2_in": b2r, "id_in": ident,
        })
    res = run_bass_kernel_spmd(nc, in_maps, core_ids=CORE_IDS)
    return np.concatenate([r["x_out"] for r in res.results], axis=0)




def emit_foldo(nc, ext, tag=""):
    sb = lambda n, s, d: nc.sbuf_tensor(tag + n, s, d)
    ps = lambda n, s, d: nc.psum_tensor(tag + n, s, d)
    o_loc = ext["o_loc"]
    woT_in = ext["woT_in"]
    P_loc = ext["P_loc"]
    r_loc = ext["r_loc"]
    x_out = ext["x_out"]
    id_in = ext["id_in"]

    with ExitStack() as ctx:
        ent = ctx.enter_context
        ots = ent(sb("ots", [128, 2, 256], BF16))
        ofm = ent(sb("ofm", [128, 2, 2, 128], BF16))
        wstg = ent(sb("wstg", [128, 2, 2048], F32))
        wob = ent(sb("wob", [128, 2, 2048], BF16))
        Pev = ent(sb("Pev", [128, 2, 2048], BF16))
        rt = ent(sb("rt", [128, 2, 2048], BF16))
        rtf = ent(sb("rtf", [128, 2, 2048], F32))
        cidf = ent(sb("cidf", [128, 128], F32))
        idb = ent(sb("idb", [128, 128], BF16))
        pt = ent(ps("pt", [128, 2, 1024], BF16))
        pP = ent(ps("pP", [128, 4, 512], F32))

        s_rc = ent(nc.semaphore("s_rc"))
        s_cn = ent(nc.semaphore("s_cn"))
        s_cv = ent(nc.semaphore("s_cv"))
        s_wl = ent(nc.semaphore("s_wl"))
        s_wb = ent(nc.semaphore("s_wb"))
        s_ol = [ent(nc.semaphore(f"s_ol{i}")) for i in range(2)]
        s_tr = ent(nc.semaphore("s_tr"))
        s_of = ent(nc.semaphore("s_of"))
        s_mm = ent(nc.semaphore("s_mm"))
        s_pe = ent(nc.semaphore("s_pe"))
        s_ps = [ent(nc.semaphore(f"s_ps{i}")) for i in range(2)]
        s_rs = ent(nc.semaphore("s_rs"))
        s_rl = [ent(nc.semaphore(f"s_rl{i}")) for i in range(2)]
        s_ac = [ent(nc.semaphore(f"s_ac{i}")) for i in range(2)]
        blk = ent(nc.Block())

        @blk.sync
        def _(q):
            q.dma_start(out=cidf[:], in_=id_in[:]).then_inc(s_cn, 16)
            for i in range(2):
                q.dma_start(out=wstg[:, i, :], in_=woT_in[i]).then_inc(s_wl, 16)
            for ch in range(64):
                if ch >= 2:
                    q.wait_ge(s_tr, 2 * (ch - 1))
                q.dma_start(out=ots[:, ch % 2, :],
                            in_=o_loc[ch]).then_inc(s_ol[ch % 2], 16)
            q.wait_ge(s_rs, 1)
            for t in range(8):
                if t >= 2:
                    q.wait_ge(s_ac[t % 2], 16 * (t // 2))
                q.dma_start(out=rt[:, t % 2, :],
                            in_=r_loc[t]).then_inc(s_rl[t % 2], 16)
            for i in range(2):
                q.wait_ge(s_ac[i], 16 * 4)

        @blk.scalar
        def _(s):
            s.wait_ge(s_wl, 32)
            for i in range(2):
                s.activation(wob[:, i, :], wstg[:, i, :],
                             AF.Copy).then_inc(s_wb, 1)
            for ch in range(64):
                s.wait_ge(s_pe, ch + 1)
                s.dma_start(out=P_loc[ch],
                            in_=Pev[:, ch % 2, :]).then_inc(s_ps[ch % 2], 16)

        @blk.tensor
        def _(t):
            t.wait_ge(s_cn, 16)
            t.wait_ge(s_cv, 1)
            for ch in range(64):
                t.wait_ge(s_ol[ch % 2], 16 * (ch // 2 + 1))
                for cc in range(2):
                    tr = 2 * ch + cc
                    if tr >= 2:
                        t.wait_ge(s_of, tr - 1)
                    t.transpose(pt[:, tr % 2, 0:128],
                                ots[:, ch % 2, cc * 128:(cc + 1) * 128],
                                idb[:]).then_inc(s_tr, 1)
                t.wait_ge(s_of, 2 * (ch + 1))
                t.wait_ge(s_wb, 2)
                if ch >= 1:
                    t.wait_ge(s_pe, ch)
                for j4 in range(4):
                    for cc in range(2):
                        ins = t.matmul(pP[:, j4, :], ofm[:, ch % 2, cc, :],
                                       wob[:, cc, j4 * 512:(j4 + 1) * 512],
                                       start=(cc == 0), stop=(cc == 1))
                    ins.then_inc(s_mm, 1)

        @blk.vector
        def _(v):
            v.wait_ge(s_cn, 16)
            v.tensor_copy(idb[:], cidf[:]).then_inc(s_cv, 1)
            for ch in range(64):
                for cc in range(2):
                    tr = 2 * ch + cc
                    v.wait_ge(s_tr, tr + 1)
                    v.tensor_copy(ofm[:, ch % 2, cc, :],
                                  pt[:, tr % 2, 0:128]).then_inc(s_of, 1)
                v.wait_ge(s_mm, 4 * (ch + 1))
                if ch >= 2:
                    v.wait_ge(s_ps[ch % 2], 16 * (ch // 2))
                v.tensor_copy(
                    Pev[:, ch % 2, :].rearrange("p (a b) -> p a b", b=512),
                    pP[:, :, :]).then_inc(s_pe, 1)
            for t in range(8):
                v.wait_ge(s_rl[t % 2], 16 * (t // 2 + 1))
                if t >= 2:
                    v.wait_ge(s_ac[t % 2], 16 * (t // 2))
                v.tensor_copy(rtf[:, t % 2, :],
                              rt[:, t % 2, :]).then_inc(s_rc, 1)

        @blk.gpsimd
        def _(g_):
            g_.wait_ge(s_ps[0], 16 * 32)
            g_.wait_ge(s_ps[1], 16 * 32)
            g_.collective_compute(
                "ReduceScatter", ALU.add, replica_groups=[CORE_IDS],
                ins=[P_loc.opt()], outs=[r_loc.opt()]).then_inc(s_rs, 1)
            g_.wait_ge(s_rs, 1)
            for t in range(8):
                g_.wait_ge(s_rc, t + 1)
                g_.dma_start(out=x_out[t * 128:(t + 1) * 128, :],
                             in_=rtf[:, t % 2, :],
                             accum_op=ALU.add).then_inc(s_ac[t % 2], 16)




def emit_foldy(nc, ext, tag=""):
    sb = lambda n, s, d: nc.sbuf_tensor(tag + n, s, d)
    ps = lambda n, s, d: nc.psum_tensor(tag + n, s, d)
    y_loc = ext["y_loc"]
    wyT_in = ext["wyT_in"]
    nwc_in = ext["nwc_in"]
    varb_in = ext["varb_in"]
    varb_out = ext["varb_out"]
    P_loc = ext["P_loc"]
    r_loc = ext["r_loc"]
    x_out = ext["x_out"]
    id_in = ext["id_in"]

    with ExitStack() as ctx:
        ent = ctx.enter_context
        yts = ent(sb("yts", [128, 2, 512], BF16))
        yt2 = ent(sb("yt2", [128, 2, 512], BF16))
        ynb = ent(sb("ynb", [128, 2, 512], BF16))
        ofm = ent(sb("ofm", [128, 2, 4, 128], BF16))
        wstg = ent(sb("wstg", [128, 4, 2048], F32))
        wob = ent(sb("wob", [128, 4, 2048], BF16))
        nwc = ent(sb("nwc", [128, 4], F32))
        vsq = ent(sb("vsq", [128, 64], F32))
        vst = ent(sb("vst", [128, 64], F32))
        rstd = ent(sb("rstd", [128, 64], F32))
        scr = ent(sb("scr", [128, 512], F32))
        scr2 = ent(sb("scr2", [128, DM], F32))
        Pev = ent(sb("Pev", [128, 2, 2048], BF16))
        rt = ent(sb("rt", [128, 2, 2048], BF16))
        xt = ent(sb("xt", [128, 2, DM], F32))
        hev = ent(sb("hev", [128, 2, DM], F32))
        cst = ent(sb("cst", [128, 8], F32))
        cidf = ent(sb("cidf", [128, 128], F32))
        idb = ent(sb("idb", [128, 128], BF16))
        pt = ent(ps("pt", [128, 2, 1024], BF16))
        pP = ent(ps("pP", [128, 4, 512], F32))

        s_cn = ent(nc.semaphore("s_cn"))
        s_cv = ent(nc.semaphore("s_cv"))
        s_wl = ent(nc.semaphore("s_wl"))
        s_nw = ent(nc.semaphore("s_nw"))
        s_wb = ent(nc.semaphore("s_wb"))
        s_yl = [ent(nc.semaphore(f"s_yl{i}")) for i in range(2)]
        s_sq = ent(nc.semaphore("s_sq"))
        s_vd = ent(nc.semaphore("s_vd"))
        s_ar = ent(nc.semaphore("s_ar"))
        s_vr = ent(nc.semaphore("s_vr"))
        s_rd = ent(nc.semaphore("s_rd"))
        s_y2 = [ent(nc.semaphore(f"s_y2{i}")) for i in range(2)]
        s_yn = ent(nc.semaphore("s_yn"))
        s_tr = ent(nc.semaphore("s_tr"))
        s_of = ent(nc.semaphore("s_of"))
        s_mm = ent(nc.semaphore("s_mm"))
        s_pe = ent(nc.semaphore("s_pe"))
        s_ps = [ent(nc.semaphore(f"s_ps{i}")) for i in range(2)]
        s_rs = ent(nc.semaphore("s_rs"))
        s_rl = [ent(nc.semaphore(f"s_rl{i}")) for i in range(2)]
        s_xl = [ent(nc.semaphore(f"s_xl{i}")) for i in range(2)]
        s_st = ent(nc.semaphore("s_st"))
        s_sc = ent(nc.semaphore("s_sc"))
        s_hv = ent(nc.semaphore("s_hv"))
        s_ac = [ent(nc.semaphore(f"s_ac{i}")) for i in range(2)]
        blk = ent(nc.Block())

        @blk.sync
        def _(q):
            q.dma_start(out=cidf[:], in_=id_in[:]).then_inc(s_cn, 16)
            q.dma_start(out=nwc[:], in_=nwc_in[:]).then_inc(s_nw, 16)
            for i in range(4):
                q.dma_start(out=wstg[:, i, :], in_=wyT_in[i]).then_inc(s_wl, 16)
            for ch in range(64):
                if ch >= 2:
                    q.wait_ge(s_sq, ch - 1)
                q.dma_start(out=yts[:, ch % 2, :],
                            in_=y_loc[ch]).then_inc(s_yl[ch % 2], 16)
            for ch in range(64):
                if ch >= 2:
                    q.wait_ge(s_yn, ch - 1)
                q.dma_start(out=yt2[:, ch % 2, :],
                            in_=y_loc[ch]).then_inc(s_y2[ch % 2], 16)
            q.wait_ge(s_rs, 1)
            for t in range(8):
                if t >= 2:
                    q.wait_ge(s_hv, 2 * (t - 1))
                q.dma_start(out=xt[:, t % 2, :],
                            in_=x_out[t * 128:(t + 1) * 128, :]
                            ).then_inc(s_xl[t % 2], 16)
                if t >= 2:
                    q.wait_ge(s_ac[t % 2], 16 * (t // 2))
                q.dma_start(out=rt[:, t % 2, :],
                            in_=r_loc[t]).then_inc(s_rl[t % 2], 16)
            for i in range(2):
                q.wait_ge(s_ac[i], 16 * 4)

        @blk.scalar
        def _(s):
            for ch in range(64):
                s.wait_ge(s_yl[ch % 2], 16 * (ch // 2 + 1))
                s.activation(scr[:], yts[:, ch % 2, :], AF.Square,
                             accum_out=vsq[:, ch:ch + 1]).then_inc(s_sq, 1)
                s.wait_ge(s_sq, ch + 1)
            s.dma_start(out=varb_in[:], in_=vsq[:]).then_inc(s_vd, 16)
            s.wait_ge(s_vr, 16)
            s.activation(cst[:, 0:1], vst[:, 0:1], AF.Copy, scale=0.0,
                         bias=1e-5).then_inc(s_rd, 1)
            s.wait_ge(s_rd, 1)
            s.activation(rstd[:], vst[:], AF.Ln, scale=1.0 / 4096.0,
                         bias=cst[:, 0:1]).then_inc(s_rd, 1)
            s.wait_ge(s_rd, 2)
            s.activation(rstd[:], rstd[:], AF.Exp, scale=-0.5).then_inc(s_rd, 1)
            s.wait_ge(s_rd, 3)
            s.nop().then_inc(s_rd, 1)
            for ch in range(64):
                s.wait_ge(s_pe, ch + 1)
                s.dma_start(out=P_loc[ch],
                            in_=Pev[:, ch % 2, :]).then_inc(s_ps[ch % 2], 16)
            n_sc = 0
            for t in range(8):
                s.wait_ge(s_xl[t % 2], 16 * (t // 2 + 1))
                if t >= 1:
                    s.wait_ge(s_hv, 2 * t)
                n_sc = emit_ln_stats(s, xt[:, t % 2, :], scr2[:], cst,
                                     LNEPS, s_sc, n_sc)
                s.nop().then_inc(s_st, 1)

        @blk.tensor
        def _(t):
            t.wait_ge(s_cn, 16)
            t.wait_ge(s_cv, 1)
            for ch in range(64):
                t.wait_ge(s_yn, ch + 1)
                for cc in range(4):
                    tr = 4 * ch + cc
                    if tr >= 2:
                        t.wait_ge(s_of, tr - 1)
                    t.transpose(pt[:, tr % 2, 0:128],
                                ynb[:, ch % 2, cc * 128:(cc + 1) * 128],
                                idb[:]).then_inc(s_tr, 1)
                t.wait_ge(s_of, 4 * (ch + 1))
                t.wait_ge(s_wb, 4)
                if ch >= 1:
                    t.wait_ge(s_pe, ch)
                for j4 in range(4):
                    for cc in range(4):
                        ins = t.matmul(pP[:, j4, :], ofm[:, ch % 2, cc, :],
                                       wob[:, cc, j4 * 512:(j4 + 1) * 512],
                                       start=(cc == 0), stop=(cc == 3))
                    ins.then_inc(s_mm, 1)

        @blk.vector
        def _(v):
            v.wait_ge(s_cn, 16)
            v.tensor_copy(idb[:], cidf[:]).then_inc(s_cv, 1)
            v.wait_ge(s_wl, 64)
            v.wait_ge(s_nw, 16)
            for i in range(4):
                v.tensor_scalar(wob[:, i, :], wstg[:, i, :], nwc[:, i:i + 1],
                                None, ALU.mult).then_inc(s_wb, 1)
            for ch in range(64):
                v.wait_ge(s_y2[ch % 2], 16 * (ch // 2 + 1))
                v.wait_ge(s_rd, 4)
                v.tensor_scalar(ynb[:, ch % 2, :], yt2[:, ch % 2, :],
                                rstd[:, ch:ch + 1], None,
                                ALU.mult).then_inc(s_yn, 1)
                for cc in range(4):
                    tr = 4 * ch + cc
                    v.wait_ge(s_tr, tr + 1)
                    v.tensor_copy(ofm[:, ch % 2, cc, :],
                                  pt[:, tr % 2, 0:128]).then_inc(s_of, 1)
                v.wait_ge(s_mm, 4 * (ch + 1))
                if ch >= 2:
                    v.wait_ge(s_ps[ch % 2], 16 * (ch // 2))
                v.tensor_copy(
                    Pev[:, ch % 2, :].rearrange("p (a b) -> p a b", b=512),
                    pP[:, :, :]).then_inc(s_pe, 1)
            for t in range(8):
                v.wait_ge(s_st, t + 1)
                if t >= 2:
                    v.wait_ge(s_ac[t % 2], 16 * (t // 2))
                v.scalar_tensor_tensor(
                    hev[:, t % 2, :], xt[:, t % 2, :], cst[:, 2:3],
                    cst[:, 5:6].broadcast_to([128, DM]),
                    ALU.subtract, ALU.mult).then_inc(s_hv, 1)
                v.wait_ge(s_rl[t % 2], 16 * (t // 2 + 1))
                v.wait_ge(s_hv, 2 * t + 1)
                v.tensor_tensor(hev[:, t % 2, :], hev[:, t % 2, :],
                                rt[:, t % 2, :], ALU.add).then_inc(s_hv, 1)

        @blk.gpsimd
        def _(g_):
            g_.wait_ge(s_vd, 16)
            g_.collective_compute(
                "AllReduce", ALU.add, replica_groups=[CORE_IDS],
                ins=[varb_in.opt()], outs=[varb_out.opt()]).then_inc(s_ar, 1)
            g_.wait_ge(s_ar, 1)
            g_.dma_start(out=vst[:], in_=varb_out[:]).then_inc(s_vr, 16)
            g_.wait_ge(s_ps[0], 16 * 32)
            g_.wait_ge(s_ps[1], 16 * 32)
            g_.collective_compute(
                "ReduceScatter", ALU.add, replica_groups=[CORE_IDS],
                ins=[P_loc.opt()], outs=[r_loc.opt()]).then_inc(s_rs, 1)
            g_.wait_ge(s_rs, 1)
            for t in range(8):
                g_.wait_ge(s_hv, 2 * (t + 1))
                g_.dma_start(out=x_out[t * 128:(t + 1) * 128, :],
                             in_=hev[:, t % 2, :],
                             accum_op=ALU.add).then_inc(s_ac[t % 2], 16)




MEGA_STAGES = ["ffn0a", "uprep_p", "b2", "foldo", "ffn0b", "tprep",
               "ffn1a", "uprep_m", "c2", "foldy", "ffn1b"]


def build_mega(upto=11, dbg=None):
    nc = bass.Bass("TRN2", target_bir_lowering=False)
    EI = lambda n, s, d: nc.dram_tensor(n, s, d, kind="ExternalInput").ap()
    x_in = EI("x_in", [TL, DM], F32)
    id_in = EI("id_in", [128, 128], F32)
    x_out = nc.dram_tensor("x_out", [TL, DM], F32, kind="ExternalOutput").ap()
    f0 = {"x_in": x_in, "x_out": x_out, "id_in": id_in,
          "wg8": EI("wg8", [32, 128, 16, 256], I8),
          "wu8": EI("wu8", [32, 128, 16, 256], I8),
          "wd8": EI("wd8", [16, 128, 64, 128], I8),
          "sg_in": EI("sg_in", [64, 128], F32),
          "su_in": EI("su_in", [64, 128], F32),
          "sd_in": EI("sd_in", [16, 128], F32)}
    ID = lambda n, s, d: nc.dram_tensor(n, s, d).ap()
    u4 = ID("u4", [4, 128, 16, 256], BF16)
    u_full = ID("u_full", [32, 128, 16, 256], BF16)
    o_loc = ID("o_loc", [64, 128, 256], BF16)
    y_loc = ID("y_loc", [64, 128, 512], BF16)
    P_loc = ID("P_loc", [64, 128, 2048], BF16)
    r_loc = ID("r_loc", [8, 128, 2048], BF16)
    t1s = ID("t1s", [8, 128, 2048], BF16)
    t2s = ID("t2s", [8, 128, 2048], BF16)
    t1f = ID("t1f", [64, 128, 2048], BF16)
    t2f = ID("t2f", [64, 128, 2048], BF16)
    vb_in = ID("vb_in", [128, 64], F32)
    vb_out = ID("vb_out", [128, 64], F32)
    tp_ccin = nc.dram_tensor("tp_ccin", [1, 16], F32)
    tp_ccout = nc.dram_tensor("tp_ccout", [1, 16], F32)

    up_p = {"x_in": x_out, "id_in": id_in, "u_out": u4, "u_full": u_full}
    up_m = {"x_in": x_out, "id_in": id_in, "u_out": u4, "u_full": u_full,
            "lnw_in": EI("lnw_in", [16, 128], F32)}
    b2e = {"u_in": u_full, "id_in": id_in, "o_out": o_loc,
           "g_dram": ID("b2g_dram", [64, 128, 256], BF16)}
    for n in ("wq_in", "wk_in", "wv_in"):
        b2e[n] = EI("b2_" + n, [2, 16, 128, 128], F32)
    b2e["wg_in"] = EI("b2_wg_in", [16, 128, 256], F32)
    b2e["wgk_in"] = EI("b2_wgk_in", [128, 16, 4], F32)
    b2e["cos_in"] = EI("b2_cos_in", [32, 128, 256], BF16)
    b2e["sin_in"] = EI("b2_sin_in", [32, 128, 256], BF16)
    b2e["swp_in"] = EI("b2_swp_in", [128, 128], BF16)
    b2e["tri_in"] = EI("b2_tri_in", [128, 128], F32)
    b2e["suf_in"] = EI("b2_suf_in", [128, 128], F32)
    b2e["one_in"] = EI("b2_one_in", [128, 128], F32)
    b2e["msk_in"] = EI("b2_msk_in", [128, 128], F32)
    b2e["id2_in"] = EI("b2_id2_in", [128, 64], F32)
    fo = {"o_loc": o_loc, "woT_in": EI("woT_in", [2, 128, 2048], F32),
          "P_loc": P_loc, "r_loc": r_loc, "x_out": x_out, "id_in": id_in}
    tpe = {"w1s_in": EI("w1s_in", [8, 128, 2048], F32),
           "w2s_in": EI("w2s_in", [8, 128, 2048], F32),
           "ones_in": EI("ones_in", [128, 128], F32),
           "t1o": t1s, "t2o": t2s, "t1f": t1f, "t2f": t2f,
           "ccin_t": tp_ccin, "ccout_t": tp_ccout}
    f1 = {"x_in": x_out, "x_out": x_out, "id_in": id_in,
          "t1f": t1f, "t2f": t2f,
          "b1c_in": EI("b1c_in", [128, 64], F32),
          "b2_in": EI("b2_in", [1, DM], F32)}
    c2e = {"u_in": u_full, "id_in": id_in, "y_out": y_loc,
           "g_dram": ID("c2g_dram", [64, 128, 512], BF16),
           "xs_dram": ID("c2xs_dram", [64, 128, 512], BF16)}
    c2e["wxs_in"] = EI("c2_wxs_in", [5, 128, 16, 128], F32)
    c2e["wg_in"] = EI("c2_wg_in", [16, 128, 512], F32)
    c2e["wdt_in"] = EI("c2_wdt_in", [16, 128, 8], F32)
    c2e["wcv_in"] = EI("c2_wcv_in", [128, 5, 4], F32)
    c2e["bcv_in"] = EI("c2_bcv_in", [128, 5], F32)
    c2e["dtb_in"] = EI("c2_dtb_in", [1, 8], F32)
    c2e["alog_in"] = EI("c2_alog_in", [1, 8], F32)
    c2e["dd_in"] = EI("c2_dd_in", [1, 8], F32)
    c2e["tri_in"] = EI("c2_tri_in", [128, 128], F32)
    c2e["suf_in"] = EI("c2_suf_in", [128, 128], F32)
    c2e["one_in"] = EI("c2_one_in", [128, 128], F32)
    c2e["msk_in"] = EI("c2_msk_in", [128, 128], F32)
    fy = {"y_loc": y_loc, "wyT_in": EI("wyT_in", [4, 128, 2048], F32),
          "nwc_in": EI("nwc_in", [128, 4], F32),
          "varb_in": vb_in, "varb_out": vb_out,
          "P_loc": P_loc, "r_loc": r_loc, "x_out": x_out, "id_in": id_in}

    stages = [
        ("ffn0a", lambda: emit_ffn0(nc, f0, tag="a_", prefill=True)),
        ("uprep_p", lambda: emit_uprep(nc, up_p, False, tag="b_", fused=True)),
        ("b2", lambda: emit_b2(nc, b2e, tag="c_")),
        ("foldo", lambda: emit_foldo(nc, fo, tag="d_")),
        ("ffn0b", lambda: emit_ffn0(nc, dict(f0, x_in=x_out), tag="e_",
                                    prefill=False)),
        ("tprep", lambda: emit_tprep(nc, tpe, tag="f_", fused=True)),
        ("ffn1a", lambda: emit_ffn1(nc, f1, tag="g_", prefill=False,
                                    fused=True)),
        ("uprep_m", lambda: emit_uprep(nc, up_m, True, tag="h_", fused=True)),
        ("c2", lambda: emit_c2(nc, c2e, tag="i_")),
        ("foldy", lambda: emit_foldy(nc, fy, tag="j_")),
        ("ffn1b", lambda: emit_ffn1(nc, f1, tag="k_", prefill=False,
                                    fused=True)),
    ]
    for name, emit in stages[:upto]:
        with nc.reset_on_exit():
            emit()
    if dbg is not None:
        dbg_map = {"u_full": u_full, "o_loc": o_loc, "y_loc": y_loc,
                   "r_loc": r_loc, "t1f": t1f, "t2f": t2f, "P_loc": P_loc}
        dsrc = dbg_map[dbg]
        dbg_out = nc.dram_tensor("dbg_out", list(dsrc.shape), dsrc.dtype,
                                 kind="ExternalOutput").ap()
        with nc.reset_on_exit():
            with nc.Block() as blk, nc.semaphore("s_dbg") as s_dbg:
                @blk.sync
                def _(q):
                    q.dma_start(out=dbg_out, in_=dsrc).then_inc(s_dbg, 16)
                    q.wait_ge(s_dbg, 16)
    return nc


def _mega_inmaps(inputs):
    x = _f32(np.asarray(inputs["L0_attn_hidden_states"]).reshape(T, DM))
    wg, wu, wd = _ffn0_wlayout(np.asarray(inputs["L0_ffn_w_gate"]),
                               np.asarray(inputs["L0_ffn_w_up"]),
                               np.asarray(inputs["L0_ffn_w_down"]))
    ident = np.eye(128, dtype=np.float32)
    b2maps = _b2_inmaps(np.asarray(inputs["L0_attn_Wq"]),
                        np.asarray(inputs["L0_attn_Wk"]),
                        np.asarray(inputs["L0_attn_Wv"]),
                        np.asarray(inputs["L0_attn_Wg"]),
                        np.asarray(inputs["L0_attn_Wgk"]))
    c2maps = _c2_inmaps(np.asarray(inputs["L1_attn_in_proj_w"]),
                        np.asarray(inputs["L1_attn_conv_w"]),
                        np.asarray(inputs["L1_attn_conv_b"]),
                        np.asarray(inputs["L1_attn_dt_bias"]),
                        np.asarray(inputs["L1_attn_A_log"]),
                        np.asarray(inputs["L1_attn_D"]))
    b1_, b2_ = _ffn1_wlayout(_f32(inputs["L1_ffn_w1"]),
                             _f32(inputs["L1_ffn_w2"]))
    WoT = _f32(inputs["L0_attn_Wo"]).T
    WyT = _f32(inputs["L1_attn_out_proj_w"]).T
    nw = _f32(inputs["L1_attn_norm_weight"])
    b1c = np.ascontiguousarray(_f32(inputs["L1_ffn_b1"]).reshape(64, 128).T)
    b2r = _f32(inputs["L1_ffn_b2"]).reshape(1, DM)
    lnw = _f32(inputs["L1_attn_ln_weight"]).reshape(16, 128)
    in_maps = []
    for c in range(NC):
        m = {"x_in": _f32(x[c * TL:(c + 1) * TL]), "id_in": ident,
             "wg8": wg, "wu8": wu, "wd8": wd,
             "sg_in": _f32(np.asarray(inputs["L0_ffn_s_gate"]).reshape(64, 128)),
             "su_in": _f32(np.asarray(inputs["L0_ffn_s_up"]).reshape(64, 128)),
             "sd_in": _f32(np.asarray(inputs["L0_ffn_s_down"]).reshape(16, 128)),
             "lnw_in": lnw,
             "w1s_in": np.ascontiguousarray(b1_[8 * c:8 * c + 8]),
             "w2s_in": np.ascontiguousarray(b2_[8 * c:8 * c + 8]),
             "ones_in": np.ones((128, 128), np.float32),
             "b1c_in": b1c, "b2_in": b2r,
             "woT_in": np.ascontiguousarray(
                 WoT[256 * c:256 * (c + 1)].reshape(2, 128, 2048)),
             "wyT_in": np.ascontiguousarray(
                 WyT[512 * c:512 * (c + 1)].reshape(4, 128, 2048)),
             "nwc_in": np.ascontiguousarray(
                 nw[512 * c:512 * (c + 1)].reshape(4, 128).T)}
        for k, v in b2maps[c].items():
            m["b2_" + k] = v
        for k, v in c2maps[c].items():
            m["c2_" + k] = v
        in_maps.append(m)
    return in_maps


def run_mega(inputs, upto=11, dbg=None):
    key = f"mega{upto}_{dbg}"
    nc = _prog(key, lambda: build_mega(upto, dbg))
    in_maps = _mega_inmaps(inputs)
    res = run_bass_kernel_spmd(nc, in_maps, core_ids=CORE_IDS)
    x = np.concatenate([r["x_out"] for r in res.results], axis=0)
    if dbg is not None:
        return x, [r["dbg_out"] for r in res.results]
    return x



def kernel(**inputs):
    x = run_mega(inputs)
    return np.ascontiguousarray(x, dtype=np.float32).reshape(1, T, DM)


def kernel_unfused(**inputs):
    x = _f32(np.asarray(inputs["L0_attn_hidden_states"]).reshape(T, DM))
    wg, wu, wd = _ffn0_wlayout(np.asarray(inputs["L0_ffn_w_gate"]),
                               np.asarray(inputs["L0_ffn_w_up"]),
                               np.asarray(inputs["L0_ffn_w_down"]))
    sg = _f32(inputs["L0_ffn_s_gate"])
    su = _f32(inputs["L0_ffn_s_up"])
    sd = _f32(inputs["L0_ffn_s_down"])
    x = run_ffn0(x, wg, wu, wd, sg, su, sd)
    u = run_uprep(x, False)
    o = run_b2(u, np.asarray(inputs["L0_attn_Wq"]),
               np.asarray(inputs["L0_attn_Wk"]),
               np.asarray(inputs["L0_attn_Wv"]),
               np.asarray(inputs["L0_attn_Wg"]),
               np.asarray(inputs["L0_attn_Wgk"]))
    x = run_tail(x, o, _tail_wlayout(_f32(inputs["L0_attn_Wo"])), False)
    x = run_ffn0(x, wg, wu, wd, sg, su, sd)
    t1, t2 = run_tprep(np.asarray(inputs["L1_ffn_w1"]),
                       np.asarray(inputs["L1_ffn_w2"]))
    b1 = _f32(inputs["L1_ffn_b1"])
    b2 = _f32(inputs["L1_ffn_b2"])
    x = run_ffn1(x, t1, t2, b1, b2)
    u = run_uprep(x, True, lnw=_f32(inputs["L1_attn_ln_weight"]))
    y = run_c2(u, np.asarray(inputs["L1_attn_in_proj_w"]),
               np.asarray(inputs["L1_attn_conv_w"]),
               np.asarray(inputs["L1_attn_conv_b"]),
               np.asarray(inputs["L1_attn_dt_bias"]),
               np.asarray(inputs["L1_attn_A_log"]),
               np.asarray(inputs["L1_attn_D"]))
    x = run_tail(x, y, _tail_wlayout(_f32(inputs["L1_attn_out_proj_w"])),
                 True, nw=_f32(inputs["L1_attn_norm_weight"]))
    x = run_ffn1(x, t1, t2, b1, b2)
    return np.ascontiguousarray(x, dtype=np.float32).reshape(1, T, DM)
